# version 01 (final)
# baseline (speedup 1.0000x reference)
import contextlib
import numpy as np

import concourse.bass as bass
import concourse.mybir as mybir
from concourse.ap import AP
from concourse.bass_utils import run_bass_kernel_spmd

F32 = mybir.dt.float32
BF16 = mybir.dt.bfloat16
AF = mybir.ActivationFunctionType
ALU = mybir.AluOpType
AX = mybir.AxisListType.X

D = 1024
S = 4096
CH = 8
P = 128
HEADS = 16
HD = 64
NPAIR = 8
TA = 512
NT = S // TA
QT = 1024
NQT = S // QT
SKN = S // 128
EPS_LN = 1e-5
EPS_AT = 1e-6

ENGS = ("pe", "act", "dve", "sp", "gp")


class Sch:
    def __init__(self, nc):
        self.nc = nc
        self.q = {e: [] for e in ENGS}
        self.sems = {}
        self.cnt = {}
        self.nblk = 0
        self.ninst = 0

    def sem(self, name, handle):
        self.sems[name] = handle
        self.cnt[name] = 0

    def emit(self, eng, fn, waits=(), inc=None, by=1, tag=""):
        for s, v in waits:
            if v > self.cnt[s]:
                raise AssertionError(
                    f"wait {s}>={v} but only {self.cnt[s]} scheduled ({tag})")
        tgt = None
        if inc is not None:
            self.cnt[inc] += by
            tgt = (inc, by)
        self.q[eng].append((tuple(waits), fn, tgt, tag))
        self.ninst += 1
        return self.cnt[inc] if inc is not None else None

    def c(self, sem):
        return self.cnt[sem]

    def flush(self):
        nc = self.nc
        self.nblk += 1
        if not any(self.q[e] for e in ENGS):
            return
        with nc.Block(f"blk{self.nblk}") as blk:
            for eng, deco in (("pe", blk.tensor), ("act", blk.scalar),
                              ("dve", blk.vector), ("sp", blk.sync),
                              ("gp", blk.gpsimd)):
                items = self.q[eng]
                if not items:
                    continue

                def body(engine, items=items):
                    for waits, fn, tgt, tag in items:
                        best = {}
                        for s, v in waits:
                            best[s] = max(best.get(s, 0), v)
                        for s, v in best.items():
                            engine.wait_ge(self.sems[s], v)
                        inst = fn(engine)
                        if tgt is not None:
                            assert inst is not None, f"no inst for inc ({tag})"
                            inst.then_inc(self.sems[tgt[0]], tgt[1])
                deco(body)
        self.q = {e: [] for e in ENGS}


def rep_c(ap2d, n_rep):
    return AP(ap2d.tensor, ap2d.offset,
              [list(ap2d.ap[0]), [0, n_rep], list(ap2d.ap[1])])


def t3(ap2d, cstride, ccount, n):
    return AP(ap2d.tensor, ap2d.offset,
              [list(ap2d.ap[0]), [cstride, ccount], [1, n]])


class MK:
    def __init__(self, phases="ABCE", dbg=(), stop=None, paranoid=False):
        self.phases = phases
        self.dbg = dbg
        self.stop = stop
        self.paranoid = paranoid
        nc = self.nc = bass.Bass("TRN2", target_bir_lowering=False,
                                 dynamic_dma_scratch_size=4096)
        self.sch = Sch(nc)
        self.stage_free = [0, 0]
        self.stage_par = 0

    def declare_hbm(self):
        nc = self.nc

        def inp(name, shape):
            return nc.dram_tensor(name, shape, F32, kind="ExternalInput")

        def scr(name, shape, dt):
            kind = "ExternalOutput" if name in self.dbg else "Internal"
            return nc.dram_tensor(name, shape, dt, kind=kind)

        self.xT = inp("xT", [D, S])
        self.wqT = inp("wqT", [D, D])
        self.wkT = inp("wkT", [D, D])
        self.wvT = inp("wvT", [D, D])
        self.woT = inp("woT", [D, D])
        self.cw1 = inp("cw1", [D, D])
        self.cw2 = inp("cw2", [D, 512])
        self.cw3 = inp("cw3", [512, 256])
        self.cw4 = inp("cw4", [P, 2])
        self.gw1 = inp("gw1", [D, 512])
        self.gw2 = inp("gw2", [512, 256])
        self.gw3 = inp("gw3", [P, 2])
        self.cb1 = inp("cb1", [P, 8])
        self.cb2 = inp("cb2", [P, 4])
        self.cb3 = inp("cb3", [P, 2])
        self.cb4 = inp("cb4", [1, 1])
        self.gb1 = inp("gb1", [P, 4])
        self.gb2 = inp("gb2", [P, 2])
        self.gb3 = inp("gb3", [1, 1])
        self.arr = inp("ar", [1, 1])
        self.yT = nc.dram_tensor("yT", [D, S], F32, kind="ExternalOutput")
        self.x1d = scr("x1d", [D, S], F32)
        self.y1d = scr("y1d", [D, S], BF16)
        self.qd = scr("qd", [D, S], BF16)
        self.x2d = scr("x2d", [D, S], F32)
        self.mu3d = scr("mu3d", [1, S], F32)
        self.var3d = scr("var3d", [1, S], F32)
        if self.stop and self.stop.startswith("att"):
            self.attnd = nc.dram_tensor("attnd", [D, QT], BF16,
                                        kind="ExternalOutput")
            self.attnd2 = nc.dram_tensor("attnd2", [D, QT], BF16,
                                         kind="ExternalOutput")
            self.wvd = nc.dram_tensor("wvd", [P, 1024], F32,
                                      kind="ExternalOutput")

    @staticmethod
    def h3(t):
        return t.ap().rearrange("(c p) s -> p c s", p=P)

    def alloc_common(self, E_):
        nc = self.nc
        for nm in ("sIN", "sGP", "sPE", "sACT", "sDVE", "sSCR", "sOUT"):
            self.sch.sem(nm, E_(nc.semaphore(nm)))
        self.ones_k = E_(nc.sbuf_tensor("ones_k", [P, 1], BF16))
        self.ones_r = E_(nc.sbuf_tensor("ones_r", [1, P], F32))
        self.nones_r = E_(nc.sbuf_tensor("nones_r", [1, P], F32))
        self.cns = E_(nc.sbuf_tensor("cns", [P, 2], F32))
        self.b_cb1 = E_(nc.sbuf_tensor("b_cb1", [P, 8], F32))
        self.b_cb2 = E_(nc.sbuf_tensor("b_cb2", [P, 4], F32))
        self.b_cb3 = E_(nc.sbuf_tensor("b_cb3", [P, 2], F32))
        self.b_cb4 = E_(nc.sbuf_tensor("b_cb4", [1, 1], F32))
        self.b_gb1 = E_(nc.sbuf_tensor("b_gb1", [P, 4], F32))
        self.b_gb2 = E_(nc.sbuf_tensor("b_gb2", [P, 2], F32))
        self.b_gb3 = E_(nc.sbuf_tensor("b_gb3", [1, 1], F32))
        self.s_ar = E_(nc.sbuf_tensor("s_ar", [1, 1], F32))
        self.srow = E_(nc.sbuf_tensor("srow", [1, 4 * TA], F32))
        self.mrs = E_(nc.sbuf_tensor("mrs", [1, NT], F32))
        self.ysums = E_(nc.sbuf_tensor("ysums", [P, CH * NT], F32))
        self.ybar = E_(nc.sbuf_tensor("ybar", [P, CH], F32))
        self.ybar_bf = E_(nc.sbuf_tensor("ybar_bf", [P, CH], BF16))
        self.g1b = E_(nc.sbuf_tensor("g1b", [P, 4], BF16))
        self.g2b = E_(nc.sbuf_tensor("g2b", [P, 2], BF16))
        self.gfs = E_(nc.sbuf_tensor("gfs", [1, 32], F32))
        self.mrbc_s = E_(nc.sbuf_tensor("mrbc_s", [P, 1], F32))

    def p0(self):
        sch = self.sch

        def consts(v):
            v.memset(self.ones_k[:, :], 1.0)
            v.memset(self.ones_r[:, :], 1.0)
            v.memset(self.nones_r[:, :], -1.0)
            v.memset(self.cns[:, 0:1], 0.125)
            return v.memset(self.cns[0:1, 1:2], -0.5)
        self.consts_done = sch.emit("dve", consts, inc="sDVE", tag="consts")
        pairs = [(self.b_cb1, self.cb1), (self.b_cb2, self.cb2),
                 (self.b_cb3, self.cb3), (self.b_cb4, self.cb4),
                 (self.b_gb1, self.gb1), (self.b_gb2, self.gb2),
                 (self.b_gb3, self.gb3), (self.s_ar, self.arr)]

        for dst, src in pairs:
            self.bias_done = sch.emit(
                "sp", lambda sy, dst=dst, src=src: sy.dma_start(dst[:, :], src[:, :]),
                inc="sIN", by=16, tag=f"bias {src.name}")

    def load_w(self, dst, src, osz, stage):
        sch = self.sch
        if osz == 1:
            nsm = src.shape[1]
            ld = sch.emit("sp", lambda sy: sy.dma_start(
                stage[:, 0:nsm], src[:, :]),
                waits=[("sDVE", self.stage_free[0])],
                inc="sIN", by=16, tag=f"w:{src.name}")
            cast = sch.emit("dve", lambda v: v.tensor_copy(
                dst[:, 0:nsm], stage[:, 0:nsm]),
                waits=[("sIN", ld)], inc="sDVE", tag=f"wc:{src.name}")
            self.stage_free[0] = cast
            return cast
        kn = src.shape[0] // P
        src3 = src.ap().rearrange("(k p) o -> p k o", p=P)
        kc_per_piece = max(1, 1024 // osz)
        cast = 0
        for k0 in range(0, kn, kc_per_piece):
            kc = min(kc_per_piece, kn - k0)
            h = (self.stage_par % 2)
            self.stage_par += 1
            soff = h * 1024
            ld = sch.emit("sp", lambda sy, k0=k0, kc=kc, soff=soff: sy.dma_start(
                AP(stage.tensor, stage.offset + soff,
                   [list(stage.ap[0]), [osz, kc], [1, osz]]),
                src3[:, k0:k0 + kc, :]),
                waits=[("sDVE", self.stage_free[h])],
                inc="sIN", by=16, tag=f"w:{src.name}.{k0}")
            cast = sch.emit("dve", lambda v, k0=k0, kc=kc, soff=soff: v.tensor_copy(
                dst[:, k0 * osz:(k0 + kc) * osz],
                AP(stage.tensor, stage.offset + soff,
                   [list(stage.ap[0]), [1, kc * osz]])),
                waits=[("sIN", ld)], inc="sDVE", tag=f"wc:{src.name}.{k0}")
            self.stage_free[h] = cast
        return cast

    def ffn(self, ph, x_src):
        nc, sch = self.nc, self.sch
        R_mu = self.srow[0:1, 0:TA]
        R_var = self.srow[0:1, TA:2 * TA]
        R_rstd = self.srow[0:1, 2 * TA:3 * TA]
        R_mr = self.srow[0:1, 3 * TA:4 * TA]
        st = contextlib.ExitStack()
        with st:
            En = st.enter_context
            state = En(nc.sbuf_tensor(f"state{ph}", [P, CH * S], F32))
            cw1b = En(nc.sbuf_tensor(f"cw1b{ph}", [P, 8 * 1024], BF16))
            cw2b = En(nc.sbuf_tensor(f"cw2b{ph}", [P, 8 * 512], BF16))
            cw3b = En(nc.sbuf_tensor(f"cw3b{ph}", [P, 4 * 256], BF16))
            cw4b = En(nc.sbuf_tensor(f"cw4b{ph}", [P, 2], BF16))
            gw1b = En(nc.sbuf_tensor(f"gw1b{ph}", [P, 8 * 512], BF16))
            gw2b = En(nc.sbuf_tensor(f"gw2b{ph}", [P, 4 * 256], BF16))
            gw3b = En(nc.sbuf_tensor(f"gw3b{ph}", [P, 2], BF16))
            x0b = En(nc.sbuf_tensor(f"x0b{ph}", [P, CH * TA], F32))
            xbf = En(nc.sbuf_tensor(f"xbf{ph}", [P, CH * TA], BF16))
            u8 = En(nc.sbuf_tensor(f"u8{ph}", [P, CH * TA], BF16))
            xsq = u8
            t1b = En(nc.sbuf_tensor(f"t1b{ph}", [P, CH * TA], BF16))
            t2b = u8
            t3b = u8
            T3OFF = 4 * TA
            tw1 = En(nc.sbuf_tensor(f"tw1{ph}", [P, TA], F32))
            tw2 = En(nc.sbuf_tensor(f"tw2{ph}", [P, TA], F32))
            pA = En(nc.psum_tensor(f"pA{ph}", [P, 512], F32))
            pB = En(nc.psum_tensor(f"pB{ph}", [P, 512], F32))
            pZ = En(nc.psum_tensor(f"pZ{ph}", [P, 512], F32))
            pP = En(nc.psum_tensor(f"pP{ph}", [P, 512], F32))
            pQ = En(nc.psum_tensor(f"pQ{ph}", [P, 512], F32))

            zc = x0b
            wstage = t1b.ap().bitcast(F32)
            self.stage_free = [0, 0]
            wdone = 0
            for dst, src, osz in ((cw1b, self.cw1, 1024), (cw2b, self.cw2, 512),
                                  (cw3b, self.cw3, 256), (cw4b, self.cw4, 1),
                                  (gw1b, self.gw1, 512), (gw2b, self.gw2, 256),
                                  (gw3b, self.gw3, 1)):
                wdone = self.load_w(dst, src, osz, wstage)
            xs3 = self.h3(x_src)

            def stt(t, c=None):
                if c is None:
                    return t3(state[:, t * TA:], S, CH, TA)
                return state[:, c * S + t * TA: c * S + (t + 1) * TA]

            def x0t():
                return t3(x0b[:, 0:], TA, CH, TA)

            def xb3():
                return t3(xbf[:, 0:], TA, CH, TA)

            def xq3():
                return t3(xsq[:, 0:], TA, CH, TA)

            have_pre = (ph == "E")
            ydve = {}
            xbf_free_pe = 0
            prev_rdone = prev_rs = prev_bc = 0
            prev_fin = 0
            for t in range(NT):
                w = [("sDVE", ydve.get(t - 1, 0))]
                ld = sch.emit("sp", lambda sy, t=t: sy.dma_start(
                    x0t(), xs3[:, :, t * TA:(t + 1) * TA]),
                    waits=w, inc="sIN", by=16, tag=f"{ph}1 x0 t{t}")
                if not have_pre:
                    cast = sch.emit("dve",
                                    lambda v: v.tensor_copy(xbf[:, :], x0b[:, :]),
                                    waits=[("sIN", ld), ("sPE", xbf_free_pe)],
                                    inc="sDVE", tag=f"{ph}1 cast")
                    sq = sch.emit("act",
                                  lambda s: s.activation(xsq[:, :], x0b[:, :], AF.Square),
                                  waits=[("sIN", ld), ("sIN", self.bias_done),
                                         ("sPE", xbf_free_pe)],
                                  inc="sACT", tag=f"{ph}1 sq")

                    def sums(te):
                        for c in range(CH):
                            te.matmul(pA[0:1, :], self.ones_k[:, :],
                                      xbf[:, c * TA:(c + 1) * TA],
                                      start=(c == 0), stop=(c == CH - 1))
                        for c in range(CH):
                            i = te.matmul(pB[0:1, :], self.ones_k[:, :],
                                          xsq[:, c * TA:(c + 1) * TA],
                                          start=(c == 0), stop=(c == CH - 1))
                        return i
                    mm = sch.emit("pe", sums,
                                  waits=[("sDVE", cast), ("sACT", sq),
                                         ("sDVE", prev_fin)],
                                  inc="sPE", tag=f"{ph}1 sums")
                    xbf_free_pe = mm

                    def fin(v):
                        v.tensor_scalar(R_mu, pA[0:1, :], 1.0 / D, None, ALU.mult)
                        v.tensor_scalar(R_var, pB[0:1, :], 1.0 / D,
                                        EPS_LN, ALU.mult, ALU.add)
                        v.tensor_tensor(R_mr, R_mu, R_mu, ALU.mult)
                        return v.tensor_tensor(R_var, R_var, R_mr, ALU.subtract)
                    fz = sch.emit("dve", fin, waits=[("sPE", mm)],
                                  inc="sDVE", tag=f"{ph}1 fin")
                    prev_fin = fz
                    rs = sch.emit("act", lambda s: (
                        s.activation(R_var, R_var, AF.Ln),
                        s.activation(R_rstd, R_var, AF.Exp,
                                     scale=self.cns[0:1, 1:2]))[-1],
                        waits=[("sDVE", fz), ("sDVE", self.consts_done)],
                        inc="sACT", tag=f"{ph}1 rstd")
                    rdone = sch.emit("dve", lambda v, t=t: (
                        v.tensor_tensor(R_mr, R_mu, R_rstd, ALU.mult),
                        v.tensor_reduce(self.mrs[0:1, t:t + 1], R_mr,
                                        AX, ALU.add))[-1],
                        waits=[("sACT", rs)], inc="sDVE", tag=f"{ph}1 mr")
                else:
                    wr = [("sSCR", self.c_scr_done)]
                    if t > 0:
                        wr += [("sDVE", prev_rdone), ("sACT", prev_rs)]
                    sch.emit("sp", lambda sy, t=t: sy.dma_start(
                        R_mu, self.mu3d[0:1, t * TA:(t + 1) * TA]),
                        waits=wr, inc="sIN", by=16, tag=f"E mu3 t{t}")
                    ldr = sch.emit("sp", lambda sy, t=t: sy.dma_start(
                        R_var, self.var3d[0:1, t * TA:(t + 1) * TA]),
                        waits=wr, inc="sIN", by=16, tag=f"E var3 t{t}")
                    wr2 = [("sIN", ldr), ("sDVE", self.consts_done)]
                    if t > 0:
                        wr2.append(("sPE", prev_bc))
                    rs = sch.emit("act", lambda s: (
                        s.activation(R_var, R_var, AF.Ln),
                        s.activation(R_rstd, R_var, AF.Exp,
                                     scale=self.cns[0:1, 1:2]))[-1],
                        waits=wr2, inc="sACT", tag="E rstd3")
                    rdone = sch.emit("dve", lambda v, t=t: (
                        v.tensor_tensor(R_mr, R_mu, R_rstd, ALU.mult),
                        v.tensor_reduce(self.mrs[0:1, t:t + 1], R_mr,
                                        AX, ALU.add))[-1],
                        waits=[("sACT", rs)], inc="sDVE", tag="E mr3")
                    prev_rdone, prev_rs = rdone, rs
                bc = sch.emit("pe", lambda te: (
                    te.matmul(pP[:, :], self.ones_r[0:1, :], R_rstd,
                              start=True, stop=True),
                    te.matmul(pQ[:, :], self.ones_r[0:1, :], R_mr,
                              start=True, stop=True))[-1],
                    waits=[("sDVE", rdone)], inc="sPE", tag=f"{ph}1 bc")
                prev_bc = bc

                def ymake(v, t=t):
                    for c in range(CH):
                        v.scalar_tensor_tensor(
                            stt(t, c), x0b[:, c * TA:(c + 1) * TA], 1.0,
                            pP[:, :], ALU.mult, ALU.mult,
                            accum_out=self.ysums[:, c * NT + t: c * NT + t + 1])
                    return v.tensor_tensor(stt(t), stt(t), rep_c(pQ[:, :], CH),
                                           ALU.subtract)
                ydve[t] = sch.emit("dve", ymake, waits=[("sPE", bc)],
                                   inc="sDVE", tag=f"{ph}1 y t{t}")

            def dump_state(after_dve):
                for t in range(NT):
                    sch.emit("sp", lambda sy, t=t: sy.dma_start(
                        self.h3(self.x1d)[:, :, t * TA:(t + 1) * TA], stt(t)),
                        waits=[("sDVE", after_dve)], inc="sSCR", by=16,
                        tag="dump")
                self.a_scr_done = sch.c("sSCR")
                sch.flush()

            if self.stop == "y":
                dump_state(ydve[NT - 1])
                return
            if self.stop == "x0":
                sch.emit("sp", lambda sy: sy.dma_start(
                    self.h3(self.x1d)[:, :, 0:TA], x0t()),
                    waits=[("sDVE", ydve[NT - 1])], inc="sSCR", by=16,
                    tag="x0dump")
                self.a_scr_done = sch.c("sSCR")
                sch.flush()
                return
            if self.stop == "sums":
                cp = sch.emit("dve", lambda v: (
                    v.tensor_copy(self.srow[0:1, 0:TA], pA[0:1, :]),
                    v.tensor_copy(self.srow[0:1, TA:2 * TA], pB[0:1, :]))[-1],
                    waits=[("sDVE", ydve[NT - 1])], inc="sDVE", tag="sumdump")
                sch.emit("sp", lambda sy: sy.dma_start(
                    self.x1d.ap()[0:1, 0:4 * TA], self.srow[0:1, :]),
                    waits=[("sDVE", cp)], inc="sSCR", by=16, tag="sumdump2")
                sch.emit("sp", lambda sy: sy.dma_start(
                    self.h3(self.y1d)[:, :, 0:TA], xb3()),
                    waits=[("sDVE", cp)], inc="sSCR", by=16, tag="xbdump")
                sch.emit("sp", lambda sy: sy.dma_start(
                    self.h3(self.y1d)[:, :, TA:2 * TA], xq3()),
                    waits=[("sDVE", cp)], inc="sSCR", by=16, tag="xqdump")
                cp2 = sch.emit("dve", lambda v: v.tensor_copy(
                    self.ybar[:, 0:1], self.ones_k[:, :]),
                    waits=[("sDVE", cp)], inc="sDVE", tag="okcp")
                sch.emit("sp", lambda sy: sy.dma_start(
                    self.x1d.ap()[0:P, 3000:3008], self.ybar[:, :]),
                    waits=[("sDVE", cp2)], inc="sSCR", by=16, tag="okdump")
                self.a_scr_done = sch.c("sSCR")
                sch.flush()
                return
            if self.stop == "rows":
                sch.emit("sp", lambda sy: sy.dma_start(
                    self.x1d.ap()[0:1, 0:4 * TA], self.srow[0:1, :]),
                    waits=[("sDVE", ydve[NT - 1])], inc="sSCR", by=16,
                    tag="rowdump")
                self.a_scr_done = sch.c("sSCR")
                sch.flush()
                return

            yb = sch.emit("dve", lambda v: ([
                v.tensor_reduce(self.ybar[:, c:c + 1],
                                self.ysums[:, c * NT:(c + 1) * NT], AX, ALU.add)
                for c in range(CH)],
                v.tensor_reduce(self.gfs[0:1, 4:5], self.mrs[0:1, :], AX, ALU.add))[-1],
                waits=[("sDVE", ydve[NT - 1])], inc="sDVE", tag="ybar")
            mb = sch.emit("pe", lambda te: te.matmul(
                pP[:, 0:1], self.ones_r[0:1, :], self.gfs[0:1, 4:5],
                start=True, stop=True),
                waits=[("sDVE", yb)], inc="sPE", tag="mrbc")
            yb2 = sch.emit("dve", lambda v: (
                v.tensor_copy(self.ybar[:, CH - 1:CH] if False else self.g1b[:, 0:1].bitcast(F32) if False else self.mrbc_s[:, 0:1], pP[:, 0:1]),
                v.tensor_scalar(self.ybar_bf[:, :], self.ybar[:, :],
                                self.mrbc_s[:, 0:1], 1.0 / S,
                                ALU.subtract, ALU.mult))[-1],
                waits=[("sPE", mb)], inc="sDVE", tag="ybar_bf")

            def gc1(te):
                for o in range(4):
                    for k in range(CH):
                        i = te.matmul(pA[:, o:o + 1],
                                      gw1b[:, k * 512 + o * P: k * 512 + o * P + P],
                                      self.ybar_bf[:, k:k + 1],
                                      start=(k == 0), stop=(k == CH - 1))
                return i
            m1 = sch.emit("pe", gc1, waits=[("sDVE", wdone), ("sDVE", yb2)],
                          inc="sPE", tag="gc1")
            a1 = sch.emit("act", lambda s: [
                s.activation(self.g1b[:, o:o + 1], pA[:, o:o + 1], AF.Gelu,
                             bias=self.b_gb1[:, o:o + 1]) for o in range(4)][-1],
                waits=[("sPE", m1), ("sIN", self.bias_done)],
                inc="sACT", tag="gc1a")

            def gc2(te):
                for o in range(2):
                    for k in range(4):
                        i = te.matmul(pB[:, o:o + 1],
                                      gw2b[:, k * 256 + o * P: k * 256 + o * P + P],
                                      self.g1b[:, k:k + 1],
                                      start=(k == 0), stop=(k == 3))
                return i
            m2 = sch.emit("pe", gc2, waits=[("sACT", a1)], inc="sPE", tag="gc2")
            a2 = sch.emit("act", lambda s: [
                s.activation(self.g2b[:, o:o + 1], pB[:, o:o + 1], AF.Gelu,
                             bias=self.b_gb2[:, o:o + 1]) for o in range(2)][-1],
                waits=[("sPE", m2)], inc="sACT", tag="gc2a")
            m3 = sch.emit("pe", lambda te: [
                te.matmul(pZ[0:1, 0:1], gw3b[:, k:k + 1], self.g2b[:, k:k + 1],
                          start=(k == 0), stop=(k == 1)) for k in range(2)][-1],
                waits=[("sACT", a2)], inc="sPE", tag="gc3")
            a3 = sch.emit("act", lambda s: s.activation(
                self.srow[0:1, 0:TA],
                AP(pZ, 0, [[pZ.ap().ap[0][0], 1], [0, TA]]), AF.Sigmoid,
                bias=self.b_gb3[0:1, 0:1]),
                waits=[("sPE", m3), ("sDVE", ydve[NT - 1])],
                inc="sACT", tag="gf")
            cc_pending = a3
            cc = None

            if self.stop == "gf":
                sch.emit("sp", lambda sy: sy.dma_start(
                    self.x1d.ap()[0:1, 0:TA], self.srow[0:1, 0:TA]),
                    waits=[("sACT", a3)], inc="sSCR", by=16, tag="gfdump")
                self.a_scr_done = sch.c("sSCR")
                sch.flush()
                return

            twist_done = dict(ydve)
            z4prev = 0
            ncyc = {"cyc0": 1, "cyc1": 2, "cyc2": 3}.get(self.stop, 3)
            for cyc in range(ncyc):
                z4d = {}
                for t in range(NT):
                    cast = sch.emit("dve",
                                    lambda v, t=t: v.tensor_copy(xb3(), stt(t)),
                                    waits=[("sDVE", twist_done[t]),
                                           ("sPE", xbf_free_pe)],
                                    inc="sDVE", tag=f"{ph}2c{cyc} cast t{t}")
                    ev = {}
                    for o in range(CH):
                        bank = (pA, pB)[o % 2]

                        def cn1(te, o=o, bank=bank):
                            for k in range(CH):
                                i = te.matmul(
                                    bank[:, :],
                                    cw1b[:, k * 1024 + o * P: k * 1024 + o * P + P],
                                    xbf[:, k * TA:(k + 1) * TA],
                                    start=(k == 0), stop=(k == CH - 1))
                            return i
                        w = [("sDVE", cast)]
                        if cyc == 0 and t == 0 and o == 0:
                            w.append(("sDVE", wdone))
                        if o - 2 in ev:
                            w.append(("sACT", ev[o - 2]))
                        mm = sch.emit("pe", cn1, waits=w, inc="sPE", tag="cn1")
                        xbf_free_pe = mm
                        ev[o] = sch.emit("act", lambda s, o=o, bank=bank:
                                         s.activation(t1b[:, o * TA:(o + 1) * TA],
                                                      bank[:, :], AF.Gelu,
                                                      bias=self.b_cb1[:, o:o + 1]),
                                         waits=[("sPE", mm)], inc="sACT", tag="cn1a")
                        if cc is None and o == 3:
                            cc = sch.emit("dve", lambda v: (
                                v.tensor_scalar(self.srow[0:1, TA:2 * TA],
                                                self.srow[0:1, 0:TA],
                                                self.s_ar[0:1, 0:1], None,
                                                ALU.mult),
                                v.tensor_scalar(self.srow[0:1, TA:2 * TA],
                                                self.srow[0:1, TA:2 * TA],
                                                1.4, 0.1, ALU.mult, ALU.add),
                                v.tensor_scalar(self.srow[0:1, 2 * TA:3 * TA],
                                                self.srow[0:1, TA:2 * TA],
                                                self.s_ar[0:1, 0:1], None,
                                                ALU.subtract))[-1],
                                waits=[("sACT", ev[3]), ("sACT", cc_pending)],
                                inc="sDVE", tag="AcBc")
                    ev2 = {}
                    for o in range(4):
                        bank = (pA, pB)[o % 2]

                        def cn2(te, o=o, bank=bank):
                            for k in range(CH):
                                i = te.matmul(
                                    bank[:, :],
                                    cw2b[:, k * 512 + o * P: k * 512 + o * P + P],
                                    t1b[:, k * TA:(k + 1) * TA],
                                    start=(k == 0), stop=(k == CH - 1))
                            return i
                        w = [("sACT", ev[CH - 1])]
                        if o - 2 in ev2:
                            w.append(("sACT", ev2[o - 2]))
                        mm = sch.emit("pe", cn2, waits=w, inc="sPE", tag="cn2")
                        ev2[o] = sch.emit("act", lambda s, o=o, bank=bank:
                                          s.activation(t2b[:, o * TA:(o + 1) * TA],
                                                       bank[:, :], AF.Gelu,
                                                       bias=self.b_cb2[:, o:o + 1]),
                                          waits=[("sPE", mm)], inc="sACT", tag="cn2a")
                    ev3 = {}
                    for o in range(2):
                        bank = (pA, pB)[o % 2]

                        def cn3(te, o=o, bank=bank):
                            for k in range(4):
                                i = te.matmul(
                                    bank[:, :],
                                    cw3b[:, k * 256 + o * P: k * 256 + o * P + P],
                                    t2b[:, k * TA:(k + 1) * TA],
                                    start=(k == 0), stop=(k == 3))
                            return i
                        mm = sch.emit("pe", cn3, waits=[("sACT", ev2[3])],
                                      inc="sPE", tag="cn3")
                        ev3[o] = sch.emit("act", lambda s, o=o, bank=bank:
                                          s.activation(
                                              t3b[:, T3OFF + o * TA: T3OFF + (o + 1) * TA],
                                              bank[:, :], AF.Gelu,
                                              bias=self.b_cb3[:, o:o + 1]),
                                          waits=[("sPE", mm)], inc="sACT", tag="cn3a")
                    m4 = sch.emit("pe", lambda te: [
                        te.matmul(pZ[0:1, :], cw4b[:, k:k + 1],
                                  t3b[:, T3OFF + k * TA: T3OFF + (k + 1) * TA],
                                  start=(k == 0), stop=(k == 1)) for k in range(2)][-1],
                        waits=[("sACT", ev3[1]),
                               ("sDVE", z4d.get(t - 1, z4prev))],
                        inc="sPE", tag="cn4")
                    z4d[t] = sch.emit("dve", lambda v, t=t: v.tensor_copy(
                        zc[0:1, t * TA:(t + 1) * TA], pZ[0:1, :]),
                        waits=[("sPE", m4)], inc="sDVE", tag="z4")
                z4prev = z4d[NT - 1]
                sg = sch.emit("act", lambda s: s.activation(
                    zc[0:1, 0:S], zc[0:1, 0:S], AF.Sigmoid,
                    bias=self.b_cb4[0:1, 0:1]),
                    waits=[("sDVE", z4d[NT - 1])], inc="sACT", tag=f"sig c{cyc}")
                crow = sch.emit("dve", lambda v: (
                    v.tensor_scalar(zc[0:1, 0:S], zc[0:1, 0:S],
                                    self.s_ar[0:1, 0:1], None, ALU.mult),
                    v.tensor_scalar(zc[0:1, 0:S], zc[0:1, 0:S], 0.6,
                                    self.srow[0:1, 2 * TA:2 * TA + 1],
                                    ALU.mult, ALU.add))[-1],
                    waits=[("sACT", sg), ("sDVE", cc)],
                    inc="sDVE", tag=f"crow c{cyc}")
                for t in range(NT):
                    w = [("sDVE", crow)]
                    if t > 0:
                        w.append(("sDVE", twist_done[t - 1]))
                    bc = sch.emit("pe", lambda te, t=t: (
                        te.matmul(pP[:, :], self.ones_r[0:1, :],
                                  zc[0:1, t * TA:(t + 1) * TA],
                                  start=True, stop=True),
                        te.matmul(pQ[:, :], self.nones_r[0:1, :],
                                  zc[0:1, t * TA:(t + 1) * TA],
                                  start=True, stop=True))[-1],
                        waits=w, inc="sPE", tag="twbc")

                    def twist(v, t=t):
                        for a in range(4):
                            hi = pP if a < 2 else pQ
                            lo = pQ if a < 2 else pP
                            v.tensor_tensor(tw1[:, :], stt(t, a + 4), hi[:, :], ALU.mult)
                            v.tensor_tensor(tw2[:, :], stt(t, a), lo[:, :], ALU.mult)
                            v.tensor_tensor(stt(t, a), stt(t, a), tw1[:, :], ALU.add)
                            i = v.tensor_tensor(stt(t, a + 4), stt(t, a + 4),
                                                tw2[:, :], ALU.add)
                        return i
                    twist_done[t] = sch.emit("dve", twist, waits=[("sPE", bc)],
                                             inc="sDVE", tag=f"twist t{t}")

            if self.stop in ("cyc0", "cyc1", "cyc2"):
                cpg = sch.emit("dve", lambda v: v.tensor_copy(
                    self.srow[0:1, 0:32], self.gfs[0:1, :]),
                    waits=[("sDVE", cc)], inc="sDVE", tag="gfsnap")
                sch.emit("gp", lambda g: g.dma_start(
                    self.y1d.ap()[0:1, 0:32], self.gfs[0:1, :]),
                    waits=[("sDVE", twist_done[NT - 1])], inc="sGP", by=16,
                    tag="gfdump-end")
                sch.emit("gp", lambda g: g.dma_start(
                    self.y1d.ap()[0:1, 32:64], self.srow[0:1, 0:32]),
                    waits=[("sDVE", twist_done[NT - 1])], inc="sGP", by=16,
                    tag="gfsnap-dump")
                dump_state(twist_done[NT - 1])
                return

            if ph == "A":
                prev_y1 = 0
                prev_y1out = 0
                prev_ld = 0
                for t in range(NT):
                    ld = sch.emit("sp", lambda sy, t=t: sy.dma_start(
                        x0t(), xs3[:, :, t * TA:(t + 1) * TA]),
                        waits=[("sDVE", twist_done[NT - 1]), ("sDVE", prev_y1)],
                        inc="sIN", by=16, tag=f"ep x0 t{t}")
                    x1 = sch.emit("dve", lambda v, t=t: v.scalar_tensor_tensor(
                        stt(t), stt(t), 0.5, x0t(), ALU.mult, ALU.add),
                        waits=[("sIN", ld)], inc="sDVE", tag=f"ep x1 t{t}")
                    sch.emit("sp", lambda sy, t=t: sy.dma_start(
                        self.h3(self.x1d)[:, :, t * TA:(t + 1) * TA], stt(t)),
                        waits=[("sDVE", x1)], inc="sSCR", by=16, tag="x1out")
                    cast = sch.emit("dve", lambda v, t=t: v.tensor_copy(
                        xb3(), stt(t)), waits=[("sDVE", x1), ("sPE", xbf_free_pe)],
                        inc="sDVE", tag="ep cast")
                    sq = sch.emit("act", lambda s, t=t: s.activation(
                        xq3(), stt(t), AF.Square),
                        waits=[("sDVE", x1), ("sPE", xbf_free_pe)],
                        inc="sACT", tag="ep sq")

                    def sums(te):
                        for c in range(CH):
                            te.matmul(pA[0:1, :], self.ones_k[:, :],
                                      xbf[:, c * TA:(c + 1) * TA],
                                      start=(c == 0), stop=(c == CH - 1))
                        for c in range(CH):
                            i = te.matmul(pB[0:1, :], self.ones_k[:, :],
                                          xsq[:, c * TA:(c + 1) * TA],
                                          start=(c == 0), stop=(c == CH - 1))
                        return i
                    mm = sch.emit("pe", sums,
                                  waits=[("sDVE", cast), ("sACT", sq),
                                         ("sDVE", prev_fin)],
                                  inc="sPE", tag="ep sums")
                    xbf_free_pe = mm

                    def fin(v):
                        v.tensor_scalar(R_mu, pA[0:1, :], 1.0 / D, None, ALU.mult)
                        v.tensor_scalar(R_var, pB[0:1, :], 1.0 / D, EPS_LN,
                                        ALU.mult, ALU.add)
                        v.tensor_tensor(R_mr, R_mu, R_mu, ALU.mult)
                        return v.tensor_tensor(R_var, R_var, R_mr, ALU.subtract)
                    fz = sch.emit("dve", fin, waits=[("sPE", mm)],
                                  inc="sDVE", tag="ep fin")
                    prev_fin = fz
                    rs = sch.emit("act", lambda s: (
                        s.activation(R_var, R_var, AF.Ln),
                        s.activation(R_rstd, R_var, AF.Exp,
                                     scale=self.cns[0:1, 1:2]))[-1],
                        waits=[("sDVE", fz)], inc="sACT", tag="ep rstd")
                    mrd = sch.emit("dve", lambda v: v.tensor_tensor(
                        R_mr, R_mu, R_rstd, ALU.mult),
                        waits=[("sACT", rs)], inc="sDVE", tag="ep mr")
                    bc = sch.emit("pe", lambda te: (
                        te.matmul(pP[:, :], self.ones_r[0:1, :], R_rstd,
                                  start=True, stop=True),
                        te.matmul(pQ[:, :], self.ones_r[0:1, :], R_mr,
                                  start=True, stop=True))[-1],
                        waits=[("sDVE", mrd)], inc="sPE", tag="ep bc")
                    y1 = sch.emit("dve", lambda v, t=t: (
                        v.scalar_tensor_tensor(x0t(), stt(t), 1.0,
                                               rep_c(pP[:, :], CH),
                                               ALU.mult, ALU.mult),
                        v.tensor_tensor(t3(t1b[:, 0:], TA, CH, TA), x0t(),
                                        rep_c(pQ[:, :], CH), ALU.subtract))[-1],
                        waits=[("sPE", bc), ("sSCR", prev_y1out)],
                        inc="sDVE", tag="y1")
                    prev_y1 = y1
                    prev_y1out = sch.emit("sp", lambda sy, t=t: sy.dma_start(
                        self.h3(self.y1d)[:, :, t * TA:(t + 1) * TA],
                        t3(t1b[:, 0:], TA, CH, TA)),
                        waits=[("sDVE", y1)], inc="sSCR", by=16, tag="y1out")
                self.a_scr_done = sch.c("sSCR")
            else:
                prev_store = 0
                for t in range(NT):
                    ld = sch.emit("sp", lambda sy, t=t: sy.dma_start(
                        x0t(), xs3[:, :, t * TA:(t + 1) * TA]),
                        waits=[("sDVE", twist_done[NT - 1]), ("sOUT", prev_store)],
                        inc="sIN", by=16, tag=f"out x2 t{t}")
                    x3 = sch.emit("dve", lambda v, t=t: v.scalar_tensor_tensor(
                        x0t(), stt(t), 0.5, x0t(), ALU.mult, ALU.add),
                        waits=[("sIN", ld)], inc="sDVE", tag=f"x3 t{t}")
                    prev_store = sch.emit("sp", lambda sy, t=t: sy.dma_start(
                        self.h3(self.yT)[:, :, t * TA:(t + 1) * TA], x0t()),
                        waits=[("sDVE", x3)], inc="sOUT", by=16, tag="yout")
            sch.flush()

    def phase_b(self):
        nc, sch = self.nc, self.sch
        st = contextlib.ExitStack()
        with st:
            En = st.enter_context
            wqb = En(nc.sbuf_tensor("wqb", [P, 8 * 1024], BF16))
            wkb = En(nc.sbuf_tensor("wkb", [P, 8 * 1024], BF16))
            wvb = En(nc.sbuf_tensor("wvb", [P, 8 * 1024], BF16))
            y1t = [En(nc.sbuf_tensor(f"y1t{i}", [P, CH * TA], BF16)) for i in (0, 1)]
            qst = [En(nc.sbuf_tensor(f"qst{i}", [P, CH * TA], BF16)) for i in (0, 1)]
            pK = En(nc.psum_tensor("pK", [P, 512], F32))
            pK2 = En(nc.psum_tensor("pK2", [P, 512], F32))
            pV = En(nc.psum_tensor("pV", [P, 512], F32))
            pV2 = En(nc.psum_tensor("pV2", [P, 512], F32))

            bstage = qst[0].ap().bitcast(F32)
            self.stage_free = [0, 0]
            wq_d = self.load_w(wqb, self.wqT, 1024, bstage)
            wk_d = self.load_w(wkb, self.wkT, 1024, bstage)
            wv_d = self.load_w(wvb, self.wvT, 1024, bstage)

            pstep0 = self.V_sb.ap().ap[0][0]
            ones_set = sch.emit("dve", lambda v: v.memset(
                AP(self.V_sb, 64, [[pstep0, P], [1040, 32], [65, 16], [1, 1]]),
                1.0), inc="sDVE", tag="V ones")

            y1h = self.h3(self.y1d)
            qh = self.h3(self.qd)
            last_pe_on_buf = {}
            qout = {}
            lastKQev = 0
            lastVev = 0
            for t in range(NT):
                tb = t % 2
                w = [("sSCR", self.a_scr_done), ("sPE", last_pe_on_buf.get(t - 2, 0))]
                ld = sch.emit("sp", lambda sy, t=t, tb=tb: sy.dma_start(
                    t3(y1t[tb][:, 0:], TA, CH, TA), y1h[:, :, t * TA:(t + 1) * TA]),
                    waits=w, inc="sIN", by=16, tag=f"B y1 t{t}")
                evQlast = 0
                for (wname, wb, wd, dst_k) in (("k", wkb, wk_d, True),
                                               ("q", wqb, wq_d, False)):
                    for o in range(CH):
                        bank = (pK, pK2)[o % 2]

                        def proj(te, wb=wb, o=o, bank=bank, tb=tb):
                            for k in range(CH):
                                i = te.matmul(
                                    bank[:, :],
                                    wb[:, k * 1024 + o * P: k * 1024 + o * P + P],
                                    y1t[tb][:, k * TA:(k + 1) * TA],
                                    start=(k == 0), stop=(k == CH - 1))
                            return i
                        w = [("sIN", ld), ("sDVE", wd), ("sACT", lastKQev)]
                        mm = sch.emit("pe", proj, waits=w, inc="sPE",
                                      tag=f"B {wname} o{o}")
                        if dst_k:
                            lastKQev = sch.emit(
                                "act", lambda s, o=o, bank=bank, t=t:
                                s.activation(
                                    self.K_sb[:, o * S + t * TA:
                                              o * S + t * TA + TA],
                                    bank[:, :], AF.Copy),
                                waits=[("sPE", mm)], inc="sACT", tag="B kev")
                        else:
                            lastKQev = sch.emit(
                                "act", lambda s, o=o, bank=bank, tb=tb:
                                s.activation(
                                    qst[tb][:, o * TA:(o + 1) * TA],
                                    bank[:, :], AF.Copy),
                                waits=[("sPE", mm), ("sDVE", wv_d),
                                       ("sSCR", qout.get(t - 2, 0))],
                                inc="sACT", tag="B qev")
                            evQlast = lastKQev
                qout[t] = sch.emit("sp", lambda sy, t=t, tb=tb: sy.dma_start(
                    qh[:, :, t * TA:(t + 1) * TA], t3(qst[tb][:, 0:], TA, CH, TA)),
                    waits=[("sACT", evQlast)], inc="sSCR", by=16, tag="B qout")
                for sub in range(4):
                    for half in range(2):
                        bank = (pV, pV2)[half]

                        def vproj(te, sub=sub, half=half, bank=bank, tb=tb):
                            for k in range(CH):
                                i = te.matmul(
                                    bank[:, :],
                                    y1t[tb][:, k * TA + sub * P: k * TA + sub * P + P],
                                    wvb[:, k * 1024 + half * 512: k * 1024 + half * 512 + 512],
                                    start=(k == 0), stop=(k == CH - 1))
                            return i
                        w = [("sIN", ld), ("sDVE", wv_d), ("sACT", lastVev)]
                        mm = sch.emit("pe", vproj, waits=w, inc="sPE",
                                      tag=f"B v s{sub}h{half}")
                        tr = t * 4 + sub
                        pstep = self.V_sb.ap().ap[0][0]
                        dstap = AP(self.V_sb, tr * 1040 + half * 8 * 65,
                                   [[pstep, P], [65, 8], [1, 64]])
                        lastVev = sch.emit(
                            "act", lambda s, dstap=dstap, bank=bank:
                            s.activation(
                                dstap,
                                bank.ap().rearrange("p (h d) -> p h d", d=64),
                                AF.Copy),
                            waits=[("sPE", mm)], inc="sACT", tag="B vev")
                last_pe_on_buf[t] = sch.c("sPE")
            self.b_done_act = sch.c("sACT")
            self.b_done_scr = sch.c("sSCR")
            self.b_vones = ones_set
            sch.flush()

    def phase_c(self):
        nc, sch = self.nc, self.sch
        st = contextlib.ExitStack()
        with st:
            En = st.enter_context
            wob = En(nc.sbuf_tensor("wob", [P, 8 * 1024], BF16))
            qbuf = [En(nc.sbuf_tensor(f"qb{i}", [P, QT], BF16)) for i in (0, 1)]
            wbuf = [En(nc.sbuf_tensor(f"wbf{i}", [P, QT], BF16)) for i in range(4)]
            o2 = En(nc.sbuf_tensor("o2", [P, QT], F32))
            rrow = En(nc.sbuf_tensor("rrow", [1, QT], F32))
            attn = En(nc.sbuf_tensor("attn", [P, CH * QT], BF16))
            x1p = [En(nc.sbuf_tensor(f"x1p{i}", [P, 512], F32)) for i in range(2)]
            x2p = [En(nc.sbuf_tensor(f"x2p{i}", [P, 512], F32)) for i in range(2)]
            x2bf = [En(nc.sbuf_tensor(f"x2bf{i}", [P, 512], BF16)) for i in (0, 1)]
            x2sq = [En(nc.sbuf_tensor(f"x2sq{i}", [P, 512], BF16)) for i in (0, 1)]
            rowC = En(nc.sbuf_tensor("rowC", [1, 2 * TA], F32))
            wstC = En(nc.sbuf_tensor("wstC", [P, 2048], F32))
            psH = En(nc.psum_tensor("psH", [P, 1024], F32))
            psG = En(nc.psum_tensor("psG", [P, 1024], F32))
            pw = [En(nc.psum_tensor(f"pw{i}", [P, 512], F32)) for i in range(4)]

            self.stage_free = [0, 0]
            wo_d = self.load_w(wob, self.woT, 1024, wstC.ap())
            qh = self.h3(self.qd)
            x1h = self.h3(self.x1d)
            x2h = self.h3(self.x2d)

            qld = {}
            rowc_out = 0
            slot_free = {}
            sig_cnt = {}
            wv_done = {}
            norm_done = {}
            wo_last_mm = 0
            x1_ld = {}
            x2_st = {}
            stat_fin = {}

            def karea(cp, sk, hi):
                off = cp * S + sk * P
                return self.K_sb[64 * hi:64 * hi + 64, off:off + P]

            def emit_qld(gidx):
                gq, gcp = divmod(gidx, NPAIR)
                if gq >= NQT or (gq, gcp) in qld:
                    return
                qb = qbuf[gidx % 2]
                w = [("sSCR", self.b_done_scr)]
                if gidx - 2 >= 0:
                    pq, pcp = divmod(gidx - 2, NPAIR)
                    assert (pq, pcp) in wv_done, "qld prefetch ordering"
                    w.append(("sPE", wv_done[(pq, pcp)]))
                qld[(gq, gcp)] = sch.emit(
                    "sp", lambda sy, gq=gq, gcp=gcp, qb=qb: sy.dma_start(
                        qb[:, :], qh[:, gcp, gq * QT:(gq + 1) * QT]),
                    waits=w, inc="sIN", by=16, tag=f"C q{gq} p{gcp}")

            emit_qld(0)
            emit_qld(1)
            for q in range(NQT):
                for cp in range(NPAIR):
                    idx = q * NPAIR + cp
                    qb = qbuf[idx % 2]
                    for sk in range(SKN):
                        if self.paranoid:
                            par = [("sPE", sch.c("sPE")),
                                   ("sACT", sch.c("sACT")),
                                   ("sDVE", sch.c("sDVE"))]
                        else:
                            par = []
                        wset = (sk % 2) * 2
                        w = [("sIN", qld[(q, cp)]), ("sACT", self.b_done_act),
                             ("sDVE", self.b_vones)]
                        if sk > 0:
                            w.append(("sACT", sig_cnt[(q, cp, sk - 1, 1)]))
                        elif idx > 0:
                            pq, pcp = divmod(idx - 1, NPAIR)
                            w.append(("sACT", sig_cnt[(pq, pcp, SKN - 1, 1)]))
                        if sk == 0 and cp == 0 and q > 0:
                            w.append(("sDVE", prev_q_dve))

                        def scores(te, cp=cp, sk=sk, qb=qb):
                            te.matmul(psH[:, 0:512], karea(cp, sk, 0),
                                      qb[0:64, 0:512], start=True, stop=True,
                                      tile_position=(0, 0))
                            te.matmul(psG[:, 0:512], karea(cp, sk, 1),
                                      qb[64:128, 0:512], start=True, stop=True,
                                      tile_position=(64, 0))
                            ih = te.matmul(psH[:, 512:1024], karea(cp, sk, 0),
                                           qb[0:64, 512:1024], start=True, stop=True,
                                           tile_position=(0, 0))
                            ig = te.matmul(psG[:, 512:1024], karea(cp, sk, 1),
                                           qb[64:128, 512:1024], start=True, stop=True,
                                           tile_position=(64, 0))
                            return (ih, ig)

                        def sc_emit(te, f=scores):
                            ih, ig = f(te)
                            return ig
                        mmsc = sch.emit("pe", sc_emit, waits=w + par,
                                        inc="sPE", tag=f"C sc q{q}p{cp}s{sk}")
                        if sk == 0:
                            emit_qld(idx + 2 - 1) if False else None
                        for hi, (ps, wb_i) in enumerate(((psH, wset), (psG, wset + 1))):
                            w = [("sPE", mmsc)]
                            if sk >= 2:
                                w.append(("sPE", wv_done.get((q, cp, sk - 2, hi), 0)))
                            elif idx > 0:
                                pq, pcp = divmod(idx - 1, NPAIR)
                                w.append(("sPE",
                                          wv_done.get((pq, pcp, SKN - 2 + sk, hi), 0)))

                            def sig(s, ps=ps, wb_i=wb_i):
                                return s.activation(wbuf[wb_i][:, :], ps[:, :],
                                                    AF.Sigmoid, scale=self.cns[:, 0:1])
                            sig_cnt[(q, cp, sk, hi)] = sch.emit(
                                "act", sig, waits=w, inc="sACT",
                                tag=f"C sig{hi}")
                        for hi in range(2):
                            h = 2 * cp + hi
                            wb_i = wset + hi
                            w = [("sACT", sig_cnt[(q, cp, sk, hi)])]
                            if sk == 0 and idx > 0:
                                pq, pcp = divmod(idx - 1, NPAIR)
                                w.append(("sDVE", norm_done.get((pq, pcp, hi), 0)))

                            def wv(te, h=h, sk=sk, wb_i=wb_i, hi=hi):
                                voff = sk * 1040 + h * 65
                                te.matmul(pw[2 * hi][0:65, :],
                                          self.V_sb[:, voff:voff + 65],
                                          wbuf[wb_i][:, 0:512],
                                          start=(sk == 0), stop=(sk == SKN - 1),
                                          skip_group_check=True)
                                return te.matmul(pw[2 * hi + 1][0:65, :],
                                                 self.V_sb[:, voff:voff + 65],
                                                 wbuf[wb_i][:, 512:1024],
                                                 start=(sk == 0), stop=(sk == SKN - 1),
                                                 skip_group_check=True)
                            wv_done[(q, cp, sk, hi)] = sch.emit(
                                "pe", wv, waits=w, inc="sPE", tag=f"C wv{hi}")
                    wv_done[(q, cp)] = sch.c("sPE")
                    emit_qld(idx + 2)
                    rbc_prev = 0
                    for hi in range(2):
                        pa, pb = pw[2 * hi], pw[2 * hi + 1]

                        def evhead(v, pa=pa, pb=pb):
                            v.tensor_scalar(rrow[0:1, 0:512], pa[64:65, :],
                                            EPS_AT, None, ALU.add)
                            v.tensor_scalar(rrow[0:1, 512:1024], pb[64:65, :],
                                            EPS_AT, None, ALU.add)
                            v.reciprocal(rrow[0:1, :], rrow[0:1, :])
                            v.tensor_copy(o2[0:64, 0:512], pa[0:64, :])
                            return v.tensor_copy(o2[0:64, 512:1024], pb[0:64, :])
                        evh = sch.emit("dve", evhead,
                                       waits=[("sPE", wv_done[(q, cp, SKN - 1, hi)]),
                                              ("sPE", rbc_prev)],
                                       inc="sDVE", tag=f"C evh{hi}")
                        rbc = sch.emit("pe", lambda te, pa=pa, pb=pb: (
                            te.matmul(pa[0:64, :], self.ones_r[0:1, 0:64],
                                      rrow[0:1, 0:512], start=True, stop=True),
                            te.matmul(pb[0:64, :], self.ones_r[0:1, 0:64],
                                      rrow[0:1, 512:1024], start=True, stop=True))[-1],
                            waits=[("sDVE", evh)], inc="sPE", tag=f"C rbc{hi}")
                        rbc_prev = rbc
                        rsl = slice(64 * hi, 64 * hi + 64)

                        def norm(v, pa=pa, pb=pb, rsl=rsl, cp=cp):
                            v.scalar_tensor_tensor(
                                attn[rsl, cp * QT: cp * QT + 512],
                                o2[0:64, 0:512], 1.0, pa[0:64, :],
                                ALU.mult, ALU.mult)
                            return v.scalar_tensor_tensor(
                                attn[rsl, cp * QT + 512: cp * QT + 1024],
                                o2[0:64, 512:1024], 1.0, pb[0:64, :],
                                ALU.mult, ALU.mult)
                        w = [("sPE", rbc)]
                        if q > 0:
                            w.append(("sPE", wo_last_mm))
                        norm_done[(q, cp, hi)] = sch.emit(
                            "dve", norm, waits=w, inc="sDVE", tag=f"C norm{hi}")
                if self.stop and self.stop.startswith("att") and q == int(self.stop[3:]):
                    wdump = [("sDVE", norm_done[(q, cp2, hi)])
                             for cp2 in range(NPAIR) for hi in range(2)]
                    sch.emit("sp", lambda sy: sy.dma_start(
                        self.attnd.ap().rearrange("(c p) s -> p c s", p=P),
                        t3(attn[:, 0:], QT, CH, QT)),
                        waits=wdump, inc="sSCR", by=16, tag="attdump")
                    cpd = sch.emit("dve", lambda v: (
                        v.tensor_copy(o2[0:65, 0:512], pw[2][0:65, :]),
                        v.tensor_copy(o2[0:65, 512:1024], pw[3][0:65, :]))[-1],
                        waits=wdump, inc="sDVE", tag="wvdump-cp")
                    sch.emit("sp", lambda sy: sy.dma_start(
                        self.wvd.ap()[:, :], o2[:, :]),
                        waits=[("sDVE", cpd)], inc="sSCR", by=16, tag="wvdump")
                    def delay(v):
                        for _ in range(20):
                            i = v.tensor_copy(o2[:, :], o2[:, :])
                        return i
                    dl = sch.emit("dve", delay, waits=[("sDVE", cpd)],
                                  inc="sDVE", tag="delay")
                    sch.emit("sp", lambda sy: sy.dma_start(
                        self.attnd2.ap().rearrange("(c p) s -> p c s", p=P),
                        t3(attn[:, 0:], QT, CH, QT)),
                        waits=[("sDVE", dl)], inc="sSCR", by=16, tag="attdump2")
                    self.c_scr_done = sch.c("sSCR")
                    sch.flush()
                    return
                attn_ready = [("sDVE", norm_done[(q, cp2, hi)])
                              for cp2 in range(NPAIR) for hi in range(2)]
                for half in range(2):
                    for o in range(CH):
                        f = q * 16 + half * CH + o
                        bank_sl = (slice(0, 512), slice(512, 1024))[o % 2]

                        def womm(te, o=o, half=half, bank_sl=bank_sl):
                            for k in range(CH):
                                i = te.matmul(
                                    psH[:, bank_sl],
                                    wob[:, k * 1024 + o * P: k * 1024 + o * P + P],
                                    attn[:, k * QT + half * 512: k * QT + half * 512 + 512],
                                    start=(k == 0), stop=(k == CH - 1))
                            return i
                        w = list(attn_ready) + [
                            ("sDVE", wo_d),
                            ("sACT", sig_cnt[(q, NPAIR - 1, SKN - 1, 1)])]
                        if f - 2 in slot_free:
                            w.append(("sDVE", slot_free[f - 2][1]))
                        mm = sch.emit("pe", womm, waits=w, inc="sPE",
                                      tag=f"C wo q{q}h{half}o{o}")
                        wo_last_mm = mm
                        j = f % 2
                        jb = f % 2
                        wx = [("sSCR", self.a_scr_done)]
                        if f - 2 in slot_free:
                            wx.append(("sDVE", slot_free[f - 2][1]))
                        ldx = sch.emit("sp", lambda sy, o=o, half=half, q=q, j=j:
                                       sy.dma_start(x1p[j][:, :],
                                                    x1h[:, o, q * QT + half * 512:
                                                        q * QT + half * 512 + 512]),
                                       waits=wx, inc="sIN", by=16, tag="C x1")
                        w2 = [("sPE", mm), ("sIN", ldx)]
                        if f - 2 in slot_free:
                            w2 += [("sSCR", slot_free[f - 2][0]),
                                   ("sACT", slot_free[f - 2][2]),
                                   ("sPE", slot_free[f - 2][3])]
                        x2c = sch.emit("dve", lambda v, j=j, bank_sl=bank_sl:
                                       v.tensor_tensor(x2p[j][:, :], psH[:, bank_sl],
                                                       x1p[j][:, :], ALU.add),
                                       waits=w2, inc="sDVE", tag="C x2")
                        x2o = sch.emit("sp", lambda sy, o=o, half=half, q=q, j=j:
                                       sy.dma_start(x2h[:, o, q * QT + half * 512:
                                                        q * QT + half * 512 + 512],
                                                    x2p[j][:, :]),
                                       waits=[("sDVE", x2c)], inc="sSCR", by=16,
                                       tag="C x2out")
                        wcb = [("sDVE", x2c)]
                        if f - 2 in slot_free:
                            wcb.append(("sPE", slot_free[f - 2][3]))
                        cst = sch.emit("dve", lambda v, j=j, jb=jb: v.tensor_copy(
                            x2bf[jb][:, :], x2p[j][:, :]),
                            waits=wcb, inc="sDVE", tag="C x2bf")
                        sqq = sch.emit("act", lambda s, j=j, jb=jb: s.activation(
                            x2sq[jb][:, :], x2p[j][:, :], AF.Square),
                            waits=wcb, inc="sACT", tag="C x2sq")

                        def statmm(te, o=o, jb=jb):
                            te.matmul(psG[0:1, 0:512], self.ones_k[:, :],
                                      x2bf[jb][:, :], start=(o == 0),
                                      stop=(o == CH - 1), skip_group_check=True)
                            return te.matmul(psG[0:1, 512:1024], self.ones_k[:, :],
                                             x2sq[jb][:, :], start=(o == 0),
                                             stop=(o == CH - 1),
                                             skip_group_check=True)
                        ws = [("sDVE", cst), ("sACT", sqq)]
                        if o == 0 and half == 1:
                            ws.append(("sDVE", stat_fin[(q, 0)]))
                        smm = sch.emit("pe", statmm, waits=ws,
                                       inc="sPE", tag="C stat")
                        slot_free[f] = (x2o, x2c, sqq, smm)
                    RC_mu = rowC[0:1, 0:512]
                    RC_var = rowC[0:1, 512:1024]

                    def finh(v):
                        v.tensor_scalar(RC_mu, psG[0:1, 0:512],
                                        1.0 / D, None, ALU.mult)
                        v.tensor_scalar(RC_var, psG[0:1, 512:1024],
                                        1.0 / D, EPS_LN, ALU.mult, ALU.add)
                        v.tensor_tensor(rrow[0:1, 0:512], RC_mu, RC_mu, ALU.mult)
                        return v.tensor_tensor(RC_var, RC_var, rrow[0:1, 0:512],
                                               ALU.subtract)
                    wfin = [("sPE", sch.c("sPE")), ("sSCR", rowc_out)]
                    stat_fin[(q, half)] = sch.emit(
                        "dve", finh, waits=wfin,
                        inc="sDVE", tag=f"C fin q{q}h{half}")
                    sch.emit("sp", lambda sy, q=q, half=half: sy.dma_start(
                        self.mu3d[0:1, q * QT + half * 512: q * QT + half * 512 + 512],
                        RC_mu), waits=[("sDVE", stat_fin[(q, half)])],
                        inc="sSCR", by=16, tag="C mu3out")
                    rowc_out = sch.emit("sp", lambda sy, q=q, half=half: sy.dma_start(
                        self.var3d[0:1, q * QT + half * 512: q * QT + half * 512 + 512],
                        RC_var), waits=[("sDVE", stat_fin[(q, half)])],
                        inc="sSCR", by=16, tag="C var3out")
                prev_q_dve = sch.c("sDVE")
            self.c_scr_done = sch.c("sSCR")
            sch.flush()

    def build(self):
        nc = self.nc
        self.declare_hbm()
        ctx = contextlib.ExitStack()
        with ctx:
            E_ = ctx.enter_context
            self.alloc_common(E_)
            self.p0()
            if "A" in self.phases:
                self.ffn("A", self.xT)
            if "B" in self.phases or "C" in self.phases:
                bc_scope = contextlib.ExitStack()
                with bc_scope:
                    self.K_sb = bc_scope.enter_context(
                        nc.sbuf_tensor("K_sb", [P, CH * S], BF16))
                    self.V_sb = bc_scope.enter_context(
                        nc.sbuf_tensor("V_sb", [P, 32 * 1040], BF16))
                    if "B" in self.phases:
                        self.phase_b()
                    if "C" in self.phases:
                        self.phase_c()
            if "E" in self.phases:
                self.ffn("E", self.x2d)
            sch = self.sch
            sch.emit("sp", lambda sy: sy.nop(),
                     waits=[("sSCR", sch.c("sSCR")), ("sOUT", sch.c("sOUT")),
                            ("sIN", sch.c("sIN")), ("sGP", sch.c("sGP"))],
                     tag="fence")
            sch.flush()
        return nc


def build_program(phases="ABCE", dbg=()):
    mk = MK(phases, dbg)
    nc = mk.build()
    return nc, mk.sch.ninst


def _prep_core_inputs(inp, b):
    x = np.asarray(inp["L0_attn_x"])

    def f32c(a):
        return np.ascontiguousarray(np.asarray(a, dtype=np.float32))
    m = {
        "xT": f32c(x[b].T),
        "wqT": f32c(np.asarray(inp["L0_attn_Wq"]).T),
        "wkT": f32c(np.asarray(inp["L0_attn_Wk"]).T),
        "wvT": f32c(np.asarray(inp["L0_attn_Wv"]).T),
        "woT": f32c(np.asarray(inp["L0_attn_Wo"]).T),
        "cw1": f32c(inp["L0_ffn_cn_w1"]),
        "cw2": f32c(inp["L0_ffn_cn_w2"]),
        "cw3": f32c(inp["L0_ffn_cn_w3"]),
        "cw4": f32c(np.asarray(inp["L0_ffn_cn_w4"]).reshape(2, 128).T),
        "gw1": f32c(inp["L0_ffn_gc_w1"]),
        "gw2": f32c(inp["L0_ffn_gc_w2"]),
        "gw3": f32c(np.asarray(inp["L0_ffn_gc_w3"]).reshape(2, 128).T),
        "cb1": f32c(np.asarray(inp["L0_ffn_cn_b1"]).reshape(8, 128).T),
        "cb2": f32c(np.asarray(inp["L0_ffn_cn_b2"]).reshape(4, 128).T),
        "cb3": f32c(np.asarray(inp["L0_ffn_cn_b3"]).reshape(2, 128).T),
        "cb4": f32c(np.asarray(inp["L0_ffn_cn_b4"]).reshape(1, 1)),
        "gb1": f32c(np.asarray(inp["L0_ffn_gc_b1"]).reshape(4, 128).T),
        "gb2": f32c(np.asarray(inp["L0_ffn_gc_b2"]).reshape(2, 128).T),
        "gb3": f32c(np.asarray(inp["L0_ffn_gc_b3"]).reshape(1, 1)),
        "ar": f32c(np.asarray(inp["L0_ffn_adaptive_range"]).reshape(1, 1)),
    }
    return m


def kernel(**inputs):
    nc, _ = build_program()
    in_maps = [_prep_core_inputs(inputs, b) for b in range(8)]
    res = run_bass_kernel_spmd(nc, in_maps, core_ids=list(range(8)))
    out = np.stack([np.asarray(res.results[b]["yT"]).T for b in range(8)], axis=0)
    return np.ascontiguousarray(out.astype(np.float32, copy=False))


# version 04
# speedup vs baseline: 1.0005x; 1.0005x over previous
import contextlib
import numpy as np

import concourse.bass as bass
import concourse.mybir as mybir
from concourse.ap import AP
from concourse.bass_utils import run_bass_kernel_spmd

F32 = mybir.dt.float32
BF16 = mybir.dt.bfloat16
AF = mybir.ActivationFunctionType
ALU = mybir.AluOpType
AX = mybir.AxisListType.X

D = 1024
S = 4096
CH = 8
P = 128
HEADS = 16
HD = 64
NPAIR = 8
TA = 512
NT = S // TA
QT = 1024
NQT = S // QT
SKN = S // 128
EPS_LN = 1e-5
EPS_AT = 1e-6

ENGS = ("pe", "act", "dve", "sp", "gp")


class Sch:
    def __init__(self, nc):
        self.nc = nc
        self.q = {e: [] for e in ENGS}
        self.sems = {}
        self.cnt = {}
        self.nblk = 0
        self.ninst = 0

    def sem(self, name, handle):
        self.sems[name] = handle
        self.cnt[name] = 0

    def emit(self, eng, fn, waits=(), inc=None, by=1, tag=""):
        for s, v in waits:
            if v > self.cnt[s]:
                raise AssertionError(
                    f"wait {s}>={v} but only {self.cnt[s]} scheduled ({tag})")
        tgt = None
        if inc is not None:
            self.cnt[inc] += by
            tgt = (inc, by)
        self.q[eng].append((tuple(waits), fn, tgt, tag))
        self.ninst += 1
        return self.cnt[inc] if inc is not None else None

    def c(self, sem):
        return self.cnt[sem]

    def flush(self):
        nc = self.nc
        self.nblk += 1
        if not any(self.q[e] for e in ENGS):
            return
        with nc.Block(f"blk{self.nblk}") as blk:
            for eng, deco in (("pe", blk.tensor), ("act", blk.scalar),
                              ("dve", blk.vector), ("sp", blk.sync),
                              ("gp", blk.gpsimd)):
                items = self.q[eng]
                if not items:
                    continue

                def body(engine, items=items):
                    for waits, fn, tgt, tag in items:
                        best = {}
                        for s, v in waits:
                            best[s] = max(best.get(s, 0), v)
                        for s, v in best.items():
                            engine.wait_ge(self.sems[s], v)
                        inst = fn(engine)
                        if tgt is not None:
                            assert inst is not None, f"no inst for inc ({tag})"
                            inst.then_inc(self.sems[tgt[0]], tgt[1])
                deco(body)
        self.q = {e: [] for e in ENGS}


def rep_c(ap2d, n_rep):
    return AP(ap2d.tensor, ap2d.offset,
              [list(ap2d.ap[0]), [0, n_rep], list(ap2d.ap[1])])


def t3(ap2d, cstride, ccount, n):
    return AP(ap2d.tensor, ap2d.offset,
              [list(ap2d.ap[0]), [cstride, ccount], [1, n]])


class MK:
    def __init__(self, phases="ABCE", dbg=(), stop=None, paranoid=False):
        self.phases = phases
        self.dbg = dbg
        self.stop = stop
        self.paranoid = paranoid
        nc = self.nc = bass.Bass("TRN2", target_bir_lowering=False,
                                 dynamic_dma_scratch_size=4096)
        self.sch = Sch(nc)
        self.stage_free = [0, 0]
        self.stage_par = 0

    def declare_hbm(self):
        nc = self.nc

        def inp(name, shape):
            return nc.dram_tensor(name, shape, F32, kind="ExternalInput")

        def scr(name, shape, dt):
            kind = "ExternalOutput" if name in self.dbg else "Internal"
            return nc.dram_tensor(name, shape, dt, kind=kind)

        self.xT = inp("xT", [D, S])
        self.wqT = inp("wqT", [D, D])
        self.wkT = inp("wkT", [D, D])
        self.wvT = inp("wvT", [D, D])
        self.woT = inp("woT", [D, D])
        self.cw1 = inp("cw1", [D, D])
        self.cw2 = inp("cw2", [D, 512])
        self.cw3 = inp("cw3", [512, 256])
        self.cw4 = inp("cw4", [P, 2])
        self.gw1 = inp("gw1", [D, 512])
        self.gw2 = inp("gw2", [512, 256])
        self.gw3 = inp("gw3", [P, 2])
        self.cb1 = inp("cb1", [P, 8])
        self.cb2 = inp("cb2", [P, 4])
        self.cb3 = inp("cb3", [P, 2])
        self.cb4 = inp("cb4", [1, 1])
        self.gb1 = inp("gb1", [P, 4])
        self.gb2 = inp("gb2", [P, 2])
        self.gb3 = inp("gb3", [1, 1])
        self.arr = inp("ar", [1, 1])
        self.cns_d = inp("cns", [P, 4])
        self.yT = nc.dram_tensor("yT", [D, S], F32, kind="ExternalOutput")
        self.x1d = scr("x1d", [D, S], F32)
        self.y1d = scr("y1d", [D, S], BF16)
        self.qd = scr("qd", [D, S], BF16)
        self.x2d = scr("x2d", [D, S], F32)
        self.mu3d = scr("mu3d", [1, S], F32)
        self.var3d = scr("var3d", [1, S], F32)
        if self.stop and self.stop.startswith("att"):
            self.attnd = nc.dram_tensor("attnd", [D, QT], BF16,
                                        kind="ExternalOutput")
            self.attnd2 = nc.dram_tensor("attnd2", [D, QT], BF16,
                                         kind="ExternalOutput")
            self.wvd = nc.dram_tensor("wvd", [P, 1024], F32,
                                      kind="ExternalOutput")

    @staticmethod
    def h3(t):
        return t.ap().rearrange("(c p) s -> p c s", p=P)

    def alloc_common(self, E_):
        nc = self.nc
        for nm in ("sIN", "sGP", "sPE", "sACT", "sDVE", "sSCR", "sOUT"):
            self.sch.sem(nm, E_(nc.semaphore(nm)))
        self.ones_k = E_(nc.sbuf_tensor("ones_k", [P, 1], BF16))
        self.ones_r = E_(nc.sbuf_tensor("ones_r", [1, P], F32))
        self.nones_r = E_(nc.sbuf_tensor("nones_r", [1, P], F32))
        self.cns = E_(nc.sbuf_tensor("cns_s", [P, 4], F32))
        self.b_cb1 = E_(nc.sbuf_tensor("b_cb1", [P, 8], F32))
        self.b_cb2 = E_(nc.sbuf_tensor("b_cb2", [P, 4], F32))
        self.b_cb3 = E_(nc.sbuf_tensor("b_cb3", [P, 2], F32))
        self.b_cb4 = E_(nc.sbuf_tensor("b_cb4", [1, 1], F32))
        self.b_gb1 = E_(nc.sbuf_tensor("b_gb1", [P, 4], F32))
        self.b_gb2 = E_(nc.sbuf_tensor("b_gb2", [P, 2], F32))
        self.b_gb3 = E_(nc.sbuf_tensor("b_gb3", [1, 1], F32))
        self.s_ar = E_(nc.sbuf_tensor("s_ar", [1, 1], F32))
        self.srow = E_(nc.sbuf_tensor("srow", [1, 4 * TA], F32))
        self.mrs = E_(nc.sbuf_tensor("mrs", [1, NT], F32))
        self.ysums = E_(nc.sbuf_tensor("ysums", [P, CH * NT], F32))
        self.ybar = E_(nc.sbuf_tensor("ybar", [P, CH], F32))
        self.ybar_bf = E_(nc.sbuf_tensor("ybar_bf", [P, CH], BF16))
        self.g1b = E_(nc.sbuf_tensor("g1b", [P, 4], BF16))
        self.g2b = E_(nc.sbuf_tensor("g2b", [P, 2], BF16))
        self.gfs = E_(nc.sbuf_tensor("gfs", [1, 32], F32))
        self.mrbc_s = E_(nc.sbuf_tensor("mrbc_s", [P, 1], F32))

    def p0(self):
        sch = self.sch

        def consts(v):
            v.memset(self.ones_k[:, :], 1.0)
            v.memset(self.ones_r[:, :], 1.0)
            return v.memset(self.nones_r[:, :], -1.0)
        self.consts_done = sch.emit("dve", consts, inc="sDVE", tag="consts")
        pairs = [(self.cns, self.cns_d), (self.b_cb1, self.cb1), (self.b_cb2, self.cb2),
                 (self.b_cb3, self.cb3), (self.b_cb4, self.cb4),
                 (self.b_gb1, self.gb1), (self.b_gb2, self.gb2),
                 (self.b_gb3, self.gb3), (self.s_ar, self.arr)]

        for dst, src in pairs:
            self.bias_done = sch.emit(
                "sp", lambda sy, dst=dst, src=src: sy.dma_start(dst[:, :], src[:, :]),
                inc="sIN", by=16, tag=f"bias {src.name}")

    def load_w(self, dst, src, osz, stage):
        sch = self.sch
        if osz == 1:
            nsm = src.shape[1]
            ld = sch.emit("sp", lambda sy: sy.dma_start(
                stage[:, 0:nsm], src[:, :]),
                waits=[("sDVE", self.stage_free[0])],
                inc="sIN", by=16, tag=f"w:{src.name}")
            cast = sch.emit("dve", lambda v: v.tensor_copy(
                dst[:, 0:nsm], stage[:, 0:nsm]),
                waits=[("sIN", ld)], inc="sDVE", tag=f"wc:{src.name}")
            self.stage_free[0] = cast
            return cast
        kn = src.shape[0] // P
        src3 = src.ap().rearrange("(k p) o -> p k o", p=P)
        kc_per_piece = max(1, 1024 // osz)
        cast = 0
        for k0 in range(0, kn, kc_per_piece):
            kc = min(kc_per_piece, kn - k0)
            h = (self.stage_par % 2)
            self.stage_par += 1
            soff = h * 1024
            ld = sch.emit("sp", lambda sy, k0=k0, kc=kc, soff=soff: sy.dma_start(
                AP(stage.tensor, stage.offset + soff,
                   [list(stage.ap[0]), [osz, kc], [1, osz]]),
                src3[:, k0:k0 + kc, :]),
                waits=[("sDVE", self.stage_free[h])],
                inc="sIN", by=16, tag=f"w:{src.name}.{k0}")
            cast = sch.emit("dve", lambda v, k0=k0, kc=kc, soff=soff: v.tensor_copy(
                dst[:, k0 * osz:(k0 + kc) * osz],
                AP(stage.tensor, stage.offset + soff,
                   [list(stage.ap[0]), [1, kc * osz]])),
                waits=[("sIN", ld)], inc="sDVE", tag=f"wc:{src.name}.{k0}")
            self.stage_free[h] = cast
        return cast

    def ffn(self, ph, x_src):
        nc, sch = self.nc, self.sch
        R_mu = self.srow[0:1, 0:TA]
        R_var = self.srow[0:1, TA:2 * TA]
        R_rstd = self.srow[0:1, 2 * TA:3 * TA]
        R_mr = self.srow[0:1, 3 * TA:4 * TA]
        R_s4 = self.srow[0:1, TA:2 * TA]
        R_ac2 = self.srow[0:1, 2 * TA:3 * TA]
        R_b4h = self.srow[0:1, 3 * TA:4 * TA]

        def bc512(tns):
            return AP(tns, 0, [[tns.ap().ap[0][0], 1], [0, TA]])
        st = contextlib.ExitStack()
        with st:
            En = st.enter_context
            state = En(nc.sbuf_tensor(f"state{ph}", [P, CH * S], F32))
            cw1b = En(nc.sbuf_tensor(f"cw1b{ph}", [P, 8 * 1024], BF16))
            cw2b = En(nc.sbuf_tensor(f"cw2b{ph}", [P, 8 * 512], BF16))
            cw3b = En(nc.sbuf_tensor(f"cw3b{ph}", [P, 4 * 256], BF16))
            cw4b = En(nc.sbuf_tensor(f"cw4b{ph}", [P, 2], BF16))
            gw1b = En(nc.sbuf_tensor(f"gw1b{ph}", [P, 8 * 512], BF16))
            gw2b = En(nc.sbuf_tensor(f"gw2b{ph}", [P, 4 * 256], BF16))
            gw3b = En(nc.sbuf_tensor(f"gw3b{ph}", [P, 2], BF16))
            x0b = En(nc.sbuf_tensor(f"x0b{ph}", [P, CH * TA], F32))
            xbf = En(nc.sbuf_tensor(f"xbf{ph}", [P, CH * TA], BF16))
            u8 = En(nc.sbuf_tensor(f"u8{ph}", [P, CH * TA], BF16))
            xsq = u8
            t1b = En(nc.sbuf_tensor(f"t1b{ph}", [P, CH * TA], BF16))
            t2b = u8
            t3b = u8
            T3OFF = 4 * TA
            tw1 = En(nc.sbuf_tensor(f"tw1{ph}", [P, TA], F32))
            tw2 = En(nc.sbuf_tensor(f"tw2{ph}", [P, TA], F32))
            pA = En(nc.psum_tensor(f"pA{ph}", [P, 512], F32))
            pB = En(nc.psum_tensor(f"pB{ph}", [P, 512], F32))
            pZ = En(nc.psum_tensor(f"pZ{ph}", [P, 512], F32))
            pP = En(nc.psum_tensor(f"pP{ph}", [P, 512], F32))
            pQ = En(nc.psum_tensor(f"pQ{ph}", [P, 512], F32))

            zc = x0b
            wstage = t1b.ap().bitcast(F32)
            self.stage_free = [0, 0]
            wdone = 0
            for dst, src, osz in ((cw1b, self.cw1, 1024), (cw2b, self.cw2, 512),
                                  (cw3b, self.cw3, 256), (cw4b, self.cw4, 1),
                                  (gw1b, self.gw1, 512), (gw2b, self.gw2, 256),
                                  (gw3b, self.gw3, 1)):
                wdone = self.load_w(dst, src, osz, wstage)
            xs3 = self.h3(x_src)

            def stt(t, c=None):
                if c is None:
                    return t3(state[:, t * TA:], S, CH, TA)
                return state[:, c * S + t * TA: c * S + (t + 1) * TA]

            def x0t():
                return t3(x0b[:, 0:], TA, CH, TA)

            def xb3():
                return t3(xbf[:, 0:], TA, CH, TA)

            def xq3():
                return t3(xsq[:, 0:], TA, CH, TA)

            have_pre = (ph == "E")
            ydve = {}
            xbf_free_pe = 0
            prev_rdone = prev_rs = prev_bc = 0
            prev_fin = 0
            for t in range(NT):
                w = [("sDVE", ydve.get(t - 1, 0))]
                ld = sch.emit("sp", lambda sy, t=t: sy.dma_start(
                    x0t(), xs3[:, :, t * TA:(t + 1) * TA]),
                    waits=w, inc="sIN", by=16, tag=f"{ph}1 x0 t{t}")
                if not have_pre:
                    cast = sch.emit("dve",
                                    lambda v: v.tensor_copy(xbf[:, :], x0b[:, :]),
                                    waits=[("sIN", ld), ("sPE", xbf_free_pe)],
                                    inc="sDVE", tag=f"{ph}1 cast")
                    sq = sch.emit("act",
                                  lambda s: s.activation(xsq[:, :], x0b[:, :], AF.Square),
                                  waits=[("sIN", ld), ("sIN", self.bias_done),
                                         ("sPE", xbf_free_pe)],
                                  inc="sACT", tag=f"{ph}1 sq")

                    def sums(te):
                        for c in range(CH):
                            te.matmul(pA[0:1, :], self.ones_k[:, :],
                                      xbf[:, c * TA:(c + 1) * TA],
                                      start=(c == 0), stop=(c == CH - 1))
                        for c in range(CH):
                            i = te.matmul(pB[0:1, :], self.ones_k[:, :],
                                          xsq[:, c * TA:(c + 1) * TA],
                                          start=(c == 0), stop=(c == CH - 1))
                        return i
                    mm = sch.emit("pe", sums,
                                  waits=[("sDVE", cast), ("sACT", sq),
                                         ("sDVE", prev_fin)],
                                  inc="sPE", tag=f"{ph}1 sums")
                    xbf_free_pe = mm

                    def fin(v):
                        v.tensor_scalar(R_mu, pA[0:1, :], 1.0 / D, None, ALU.mult)
                        v.tensor_scalar(R_var, pB[0:1, :], 1.0 / D,
                                        EPS_LN, ALU.mult, ALU.add)
                        v.tensor_tensor(R_mr, R_mu, R_mu, ALU.mult)
                        return v.tensor_tensor(R_var, R_var, R_mr, ALU.subtract)
                    fz = sch.emit("dve", fin, waits=[("sPE", mm)],
                                  inc="sDVE", tag=f"{ph}1 fin")
                    prev_fin = fz
                    rs = sch.emit("act", lambda s: (
                        s.activation(R_var, R_var, AF.Ln),
                        s.activation(R_rstd, R_var, AF.Exp,
                                     scale=self.cns[0:1, 1:2]))[-1],
                        waits=[("sDVE", fz), ("sDVE", self.consts_done)],
                        inc="sACT", tag=f"{ph}1 rstd")
                    rdone = sch.emit("dve", lambda v, t=t: (
                        v.tensor_tensor(R_mr, R_mu, R_rstd, ALU.mult),
                        v.tensor_reduce(self.mrs[0:1, t:t + 1], R_mr,
                                        AX, ALU.add))[-1],
                        waits=[("sACT", rs)], inc="sDVE", tag=f"{ph}1 mr")
                else:
                    wr = [("sSCR", self.c_scr_done)]
                    if t > 0:
                        wr += [("sDVE", prev_rdone), ("sACT", prev_rs)]
                    sch.emit("sp", lambda sy, t=t: sy.dma_start(
                        R_mu, self.mu3d[0:1, t * TA:(t + 1) * TA]),
                        waits=wr, inc="sIN", by=16, tag=f"E mu3 t{t}")
                    ldr = sch.emit("sp", lambda sy, t=t: sy.dma_start(
                        R_var, self.var3d[0:1, t * TA:(t + 1) * TA]),
                        waits=wr, inc="sIN", by=16, tag=f"E var3 t{t}")
                    wr2 = [("sIN", ldr), ("sDVE", self.consts_done)]
                    if t > 0:
                        wr2.append(("sPE", prev_bc))
                    rs = sch.emit("act", lambda s: (
                        s.activation(R_var, R_var, AF.Ln),
                        s.activation(R_rstd, R_var, AF.Exp,
                                     scale=self.cns[0:1, 1:2]))[-1],
                        waits=wr2, inc="sACT", tag="E rstd3")
                    rdone = sch.emit("dve", lambda v, t=t: (
                        v.tensor_tensor(R_mr, R_mu, R_rstd, ALU.mult),
                        v.tensor_reduce(self.mrs[0:1, t:t + 1], R_mr,
                                        AX, ALU.add))[-1],
                        waits=[("sACT", rs)], inc="sDVE", tag="E mr3")
                    prev_rdone, prev_rs = rdone, rs
                bc = sch.emit("pe", lambda te: (
                    te.matmul(pP[:, :], self.ones_r[0:1, :], R_rstd,
                              start=True, stop=True),
                    te.matmul(pQ[:, :], self.ones_r[0:1, :], R_mr,
                              start=True, stop=True))[-1],
                    waits=[("sDVE", rdone)], inc="sPE", tag=f"{ph}1 bc")
                prev_bc = bc

                def ymake(v, t=t):
                    for c in range(CH):
                        v.scalar_tensor_tensor(
                            stt(t, c), x0b[:, c * TA:(c + 1) * TA], 1.0,
                            pP[:, :], ALU.mult, ALU.mult,
                            accum_out=self.ysums[:, c * NT + t: c * NT + t + 1])
                    return v.tensor_tensor(stt(t), stt(t), rep_c(pQ[:, :], CH),
                                           ALU.subtract)
                ydve[t] = sch.emit("dve", ymake, waits=[("sPE", bc)],
                                   inc="sDVE", tag=f"{ph}1 y t{t}")

            def dump_state(after_dve):
                for t in range(NT):
                    sch.emit("sp", lambda sy, t=t: sy.dma_start(
                        self.h3(self.x1d)[:, :, t * TA:(t + 1) * TA], stt(t)),
                        waits=[("sDVE", after_dve)], inc="sSCR", by=16,
                        tag="dump")
                self.a_scr_done = sch.c("sSCR")
                sch.flush()

            if self.stop == "y":
                dump_state(ydve[NT - 1])
                return
            if self.stop == "x0":
                sch.emit("sp", lambda sy: sy.dma_start(
                    self.h3(self.x1d)[:, :, 0:TA], x0t()),
                    waits=[("sDVE", ydve[NT - 1])], inc="sSCR", by=16,
                    tag="x0dump")
                self.a_scr_done = sch.c("sSCR")
                sch.flush()
                return
            if self.stop == "sums":
                cp = sch.emit("dve", lambda v: (
                    v.tensor_copy(self.srow[0:1, 0:TA], pA[0:1, :]),
                    v.tensor_copy(self.srow[0:1, TA:2 * TA], pB[0:1, :]))[-1],
                    waits=[("sDVE", ydve[NT - 1])], inc="sDVE", tag="sumdump")
                sch.emit("sp", lambda sy: sy.dma_start(
                    self.x1d.ap()[0:1, 0:4 * TA], self.srow[0:1, :]),
                    waits=[("sDVE", cp)], inc="sSCR", by=16, tag="sumdump2")
                sch.emit("sp", lambda sy: sy.dma_start(
                    self.h3(self.y1d)[:, :, 0:TA], xb3()),
                    waits=[("sDVE", cp)], inc="sSCR", by=16, tag="xbdump")
                sch.emit("sp", lambda sy: sy.dma_start(
                    self.h3(self.y1d)[:, :, TA:2 * TA], xq3()),
                    waits=[("sDVE", cp)], inc="sSCR", by=16, tag="xqdump")
                cp2 = sch.emit("dve", lambda v: v.tensor_copy(
                    self.ybar[:, 0:1], self.ones_k[:, :]),
                    waits=[("sDVE", cp)], inc="sDVE", tag="okcp")
                sch.emit("sp", lambda sy: sy.dma_start(
                    self.x1d.ap()[0:P, 3000:3008], self.ybar[:, :]),
                    waits=[("sDVE", cp2)], inc="sSCR", by=16, tag="okdump")
                self.a_scr_done = sch.c("sSCR")
                sch.flush()
                return
            if self.stop == "rows":
                sch.emit("sp", lambda sy: sy.dma_start(
                    self.x1d.ap()[0:1, 0:4 * TA], self.srow[0:1, :]),
                    waits=[("sDVE", ydve[NT - 1])], inc="sSCR", by=16,
                    tag="rowdump")
                self.a_scr_done = sch.c("sSCR")
                sch.flush()
                return

            halfrows = sch.emit("dve", lambda v: (
                v.tensor_scalar(R_s4, self.srow[0:1, 0:TA], 0.0,
                                self.b_gb3[0:1, 0:1], ALU.mult, ALU.add),
                v.tensor_scalar(R_s4, R_s4, 0.5, None, ALU.mult),
                v.tensor_scalar(R_b4h, self.srow[0:1, 0:TA], 0.0,
                                self.b_cb4[0:1, 0:1], ALU.mult, ALU.add),
                v.tensor_scalar(R_b4h, R_b4h, 0.5, None, ALU.mult))[-1],
                waits=[("sIN", self.bias_done), ("sDVE", ydve[NT - 1]),
                       ("sPE", sch.c("sPE"))],
                inc="sDVE", tag=f"{ph} halfrows")
            yb = sch.emit("dve", lambda v: ([
                v.tensor_reduce(self.ybar[:, c:c + 1],
                                self.ysums[:, c * NT:(c + 1) * NT], AX, ALU.add)
                for c in range(CH)],
                v.tensor_reduce(self.gfs[0:1, 4:5], self.mrs[0:1, :], AX, ALU.add))[-1],
                waits=[("sDVE", ydve[NT - 1])], inc="sDVE", tag="ybar")
            mb = sch.emit("pe", lambda te: te.matmul(
                pP[:, 0:1], self.ones_r[0:1, :], self.gfs[0:1, 4:5],
                start=True, stop=True),
                waits=[("sDVE", yb)], inc="sPE", tag="mrbc")
            yb2 = sch.emit("dve", lambda v: (
                v.tensor_copy(self.ybar[:, CH - 1:CH] if False else self.g1b[:, 0:1].bitcast(F32) if False else self.mrbc_s[:, 0:1], pP[:, 0:1]),
                v.tensor_scalar(self.ybar_bf[:, :], self.ybar[:, :],
                                self.mrbc_s[:, 0:1], 1.0 / S,
                                ALU.subtract, ALU.mult))[-1],
                waits=[("sPE", mb)], inc="sDVE", tag="ybar_bf")

            def gc1(te):
                for o in range(4):
                    for k in range(CH):
                        i = te.matmul(pA[:, o:o + 1],
                                      gw1b[:, k * 512 + o * P: k * 512 + o * P + P],
                                      self.ybar_bf[:, k:k + 1],
                                      start=(k == 0), stop=(k == CH - 1))
                return i
            m1 = sch.emit("pe", gc1, waits=[("sDVE", wdone), ("sDVE", yb2)],
                          inc="sPE", tag="gc1")
            a1 = sch.emit("act", lambda s: [
                s.activation(self.g1b[:, o:o + 1], pA[:, o:o + 1], AF.Gelu,
                             bias=self.b_gb1[:, o:o + 1]) for o in range(4)][-1],
                waits=[("sPE", m1), ("sIN", self.bias_done)],
                inc="sACT", tag="gc1a")

            def gc2(te):
                for o in range(2):
                    for k in range(4):
                        i = te.matmul(pB[:, o:o + 1],
                                      gw2b[:, k * 256 + o * P: k * 256 + o * P + P],
                                      self.g1b[:, k:k + 1],
                                      start=(k == 0), stop=(k == 3))
                return i
            m2 = sch.emit("pe", gc2, waits=[("sACT", a1)], inc="sPE", tag="gc2")
            a2 = sch.emit("act", lambda s: [
                s.activation(self.g2b[:, o:o + 1], pB[:, o:o + 1], AF.Gelu,
                             bias=self.b_gb2[:, o:o + 1]) for o in range(2)][-1],
                waits=[("sPE", m2)], inc="sACT", tag="gc2a")
            m3 = sch.emit("pe", lambda te: [
                te.matmul(pZ[0:1, 0:1], gw3b[:, k:k + 1], self.g2b[:, k:k + 1],
                          start=(k == 0), stop=(k == 1)) for k in range(2)][-1],
                waits=[("sACT", a2)], inc="sPE", tag="gc3")
            a3 = sch.emit("act", lambda s: s.activation(
                self.srow[0:1, 0:TA],
                AP(pZ, 0, [[pZ.ap().ap[0][0], 1], [0, TA]]), AF.Sigmoid,
                bias=self.b_gb3[0:1, 0:1]),
                waits=[("sPE", m3), ("sDVE", ydve[NT - 1]),
                       ("sDVE", halfrows)],
                inc="sACT", tag="gf")
            cc_pending = a3
            cc = None

            if self.stop == "gf":
                sch.emit("sp", lambda sy: sy.dma_start(
                    self.x1d.ap()[0:1, 0:TA], self.srow[0:1, 0:TA]),
                    waits=[("sACT", a3)], inc="sSCR", by=16, tag="gfdump")
                self.a_scr_done = sch.c("sSCR")
                sch.flush()
                return

            twist_done = dict(ydve)
            z4prev = 0
            bc_prev = {}
            ncyc = {"cyc0": 1, "cyc1": 2, "cyc2": 3}.get(self.stop, 3)
            for cyc in range(ncyc):
                z4d = {}
                z4prev_next = None
                for t in range(NT):
                    cast = sch.emit("dve",
                                    lambda v, t=t: v.tensor_copy(xb3(), stt(t)),
                                    waits=[("sDVE", twist_done[t]),
                                           ("sPE", xbf_free_pe)],
                                    inc="sDVE", tag=f"{ph}2c{cyc} cast t{t}")
                    ev = {}
                    for o in range(CH):
                        bank = (pA, pB)[o % 2]

                        def cn1(te, o=o, bank=bank):
                            for k in range(CH):
                                i = te.matmul(
                                    bank[:, :],
                                    cw1b[:, k * 1024 + o * P: k * 1024 + o * P + P],
                                    xbf[:, k * TA:(k + 1) * TA],
                                    start=(k == 0), stop=(k == CH - 1))
                            return i
                        w = [("sDVE", cast)]
                        if cyc == 0 and t == 0 and o == 0:
                            w.append(("sDVE", wdone))
                        if o - 2 in ev:
                            w.append(("sACT", ev[o - 2]))
                        mm = sch.emit("pe", cn1, waits=w, inc="sPE", tag="cn1")
                        xbf_free_pe = mm
                        ev[o] = sch.emit("act", lambda s, o=o, bank=bank:
                                         s.activation(t1b[:, o * TA:(o + 1) * TA],
                                                      bank[:, :], AF.Gelu,
                                                      bias=self.b_cb1[:, o:o + 1]),
                                         waits=[("sPE", mm)], inc="sACT", tag="cn1a")
                        if cc is None and o == 3:
                            cc = sch.emit("dve", lambda v: (
                                v.tensor_scalar(R_s4, self.srow[0:1, 0:TA],
                                                0.0, self.s_ar[0:1, 0:1],
                                                ALU.mult, ALU.add),
                                v.tensor_scalar(R_ac2, self.srow[0:1, 0:TA],
                                                self.s_ar[0:1, 0:1], None,
                                                ALU.mult),
                                v.tensor_scalar(R_ac2, R_ac2,
                                                1.4, 0.1, ALU.mult, ALU.add),
                                v.scalar_tensor_tensor(R_ac2, R_s4, -0.7,
                                                       R_ac2, ALU.mult,
                                                       ALU.add),
                                v.tensor_scalar(R_s4, R_s4,
                                                0.3, None, ALU.mult))[-1],
                                waits=[("sACT", ev[3]), ("sACT", cc_pending)],
                                inc="sDVE", tag="AcBc")
                    ev2 = {}
                    for o in range(4):
                        bank = (pA, pB)[o % 2]

                        def cn2(te, o=o, bank=bank):
                            for k in range(CH):
                                i = te.matmul(
                                    bank[:, :],
                                    cw2b[:, k * 512 + o * P: k * 512 + o * P + P],
                                    t1b[:, k * TA:(k + 1) * TA],
                                    start=(k == 0), stop=(k == CH - 1))
                            return i
                        w = [("sACT", ev[CH - 1])]
                        if o - 2 in ev2:
                            w.append(("sACT", ev2[o - 2]))
                        mm = sch.emit("pe", cn2, waits=w, inc="sPE", tag="cn2")
                        ev2[o] = sch.emit("act", lambda s, o=o, bank=bank:
                                          s.activation(t2b[:, o * TA:(o + 1) * TA],
                                                       bank[:, :], AF.Gelu,
                                                       bias=self.b_cb2[:, o:o + 1]),
                                          waits=[("sPE", mm)], inc="sACT", tag="cn2a")
                    ev3 = {}
                    for o in range(2):
                        bank = (pA, pB)[o % 2]

                        def cn3(te, o=o, bank=bank):
                            for k in range(4):
                                i = te.matmul(
                                    bank[:, :],
                                    cw3b[:, k * 256 + o * P: k * 256 + o * P + P],
                                    t2b[:, k * TA:(k + 1) * TA],
                                    start=(k == 0), stop=(k == 3))
                            return i
                        mm = sch.emit("pe", cn3, waits=[("sACT", ev2[3])],
                                      inc="sPE", tag="cn3")
                        ev3[o] = sch.emit("act", lambda s, o=o, bank=bank:
                                          s.activation(
                                              t3b[:, T3OFF + o * TA: T3OFF + (o + 1) * TA],
                                              bank[:, :], AF.Gelu,
                                              bias=self.b_cb3[:, o:o + 1]),
                                          waits=[("sPE", mm)], inc="sACT", tag="cn3a")
                    m4 = sch.emit("pe", lambda te: [
                        te.matmul(pZ[0:1, :], cw4b[:, k:k + 1],
                                  t3b[:, T3OFF + k * TA: T3OFF + (k + 1) * TA],
                                  start=(k == 0), stop=(k == 1)) for k in range(2)][-1],
                        waits=[("sACT", ev3[1]),
                               ("sACT", z4d.get(t - 1, z4prev))],
                        inc="sPE", tag="cn4")
                    wt = [("sPE", m4), ("sDVE", halfrows)]
                    if bc_prev.get(t):
                        wt.append(("sPE", bc_prev[t]))
                    zev = sch.emit("dve", lambda v, t=t: v.tensor_copy(
                        zc[0:1, t * TA:(t + 1) * TA], pZ[0:1, :]),
                        waits=wt, inc="sDVE", tag="z4ev")
                    z4d[t] = sch.emit("act", lambda s, t=t: s.activation(
                        zc[0:1, t * TA:(t + 1) * TA],
                        zc[0:1, t * TA:(t + 1) * TA], AF.Tanh,
                        scale=self.cns[0:1, 2:3], bias=R_b4h[0:1, 0:1]),
                        waits=[("sDVE", zev)], inc="sACT", tag="z4tanh")
                    crow_t = sch.emit("dve", lambda v, t=t: v.tensor_scalar(
                        zc[0:1, t * TA:(t + 1) * TA],
                        zc[0:1, t * TA:(t + 1) * TA],
                        R_s4[0:1, 0:1], R_ac2[0:1, 0:1], ALU.mult, ALU.add),
                        waits=[("sACT", z4d[t]), ("sDVE", cc)],
                        inc="sDVE", tag="crow_t")
                    w = [("sDVE", crow_t)]
                    if t > 0:
                        w.append(("sDVE", twist_done[t - 1]))
                    elif cyc > 0:
                        w.append(("sDVE", twist_done[NT - 1]))
                    bc = sch.emit("pe", lambda te, t=t: (
                        te.matmul(pP[:, :], self.ones_r[0:1, :],
                                  zc[0:1, t * TA:(t + 1) * TA],
                                  start=True, stop=True),
                        te.matmul(pQ[:, :], self.nones_r[0:1, :],
                                  zc[0:1, t * TA:(t + 1) * TA],
                                  start=True, stop=True))[-1],
                        waits=w, inc="sPE", tag="twbc")
                    bc_prev[t] = bc

                    def twist(v, t=t):
                        for a in range(4):
                            hi = pP if a < 2 else pQ
                            lo = pQ if a < 2 else pP
                            v.tensor_tensor(tw1[:, :], stt(t, a + 4), hi[:, :], ALU.mult)
                            v.tensor_tensor(tw2[:, :], stt(t, a), lo[:, :], ALU.mult)
                            v.tensor_tensor(stt(t, a), stt(t, a), tw1[:, :], ALU.add)
                            i = v.tensor_tensor(stt(t, a + 4), stt(t, a + 4),
                                                tw2[:, :], ALU.add)
                        return i
                    twist_done[t] = sch.emit("dve", twist, waits=[("sPE", bc)],
                                             inc="sDVE", tag=f"twist t{t}")
                    if t == NT - 1:
                        z4prev = z4d[t]

            if self.stop in ("cyc0", "cyc1", "cyc2"):
                sch.emit("sp", lambda sy: sy.dma_start(
                    self.mu3d[0:1, 0:4 * TA], self.srow[0:1, :]),
                    waits=[("sDVE", twist_done[NT - 1])], inc="sSCR", by=16,
                    tag="srowdump")
                dump_state(twist_done[NT - 1])
                return

            if ph == "A":
                prev_y1 = 0
                prev_y1out = 0
                prev_ld = 0
                for t in range(NT):
                    ld = sch.emit("sp", lambda sy, t=t: sy.dma_start(
                        x0t(), xs3[:, :, t * TA:(t + 1) * TA]),
                        waits=[("sDVE", twist_done[NT - 1]), ("sDVE", prev_y1)],
                        inc="sIN", by=16, tag=f"ep x0 t{t}")
                    x1 = sch.emit("dve", lambda v, t=t: v.scalar_tensor_tensor(
                        stt(t), stt(t), 0.5, x0t(), ALU.mult, ALU.add),
                        waits=[("sIN", ld)], inc="sDVE", tag=f"ep x1 t{t}")
                    sch.emit("sp", lambda sy, t=t: sy.dma_start(
                        self.h3(self.x1d)[:, :, t * TA:(t + 1) * TA], stt(t)),
                        waits=[("sDVE", x1)], inc="sSCR", by=16, tag="x1out")
                    cast = sch.emit("dve", lambda v, t=t: v.tensor_copy(
                        xb3(), stt(t)), waits=[("sDVE", x1), ("sPE", xbf_free_pe)],
                        inc="sDVE", tag="ep cast")
                    sq = sch.emit("act", lambda s, t=t: s.activation(
                        xq3(), stt(t), AF.Square),
                        waits=[("sDVE", x1), ("sPE", xbf_free_pe)],
                        inc="sACT", tag="ep sq")

                    def sums(te):
                        for c in range(CH):
                            te.matmul(pA[0:1, :], self.ones_k[:, :],
                                      xbf[:, c * TA:(c + 1) * TA],
                                      start=(c == 0), stop=(c == CH - 1))
                        for c in range(CH):
                            i = te.matmul(pB[0:1, :], self.ones_k[:, :],
                                          xsq[:, c * TA:(c + 1) * TA],
                                          start=(c == 0), stop=(c == CH - 1))
                        return i
                    mm = sch.emit("pe", sums,
                                  waits=[("sDVE", cast), ("sACT", sq),
                                         ("sDVE", prev_fin)],
                                  inc="sPE", tag="ep sums")
                    xbf_free_pe = mm

                    def fin(v):
                        v.tensor_scalar(R_mu, pA[0:1, :], 1.0 / D, None, ALU.mult)
                        v.tensor_scalar(R_var, pB[0:1, :], 1.0 / D, EPS_LN,
                                        ALU.mult, ALU.add)
                        v.tensor_tensor(R_mr, R_mu, R_mu, ALU.mult)
                        return v.tensor_tensor(R_var, R_var, R_mr, ALU.subtract)
                    fz = sch.emit("dve", fin, waits=[("sPE", mm)],
                                  inc="sDVE", tag="ep fin")
                    prev_fin = fz
                    rs = sch.emit("act", lambda s: (
                        s.activation(R_var, R_var, AF.Ln),
                        s.activation(R_rstd, R_var, AF.Exp,
                                     scale=self.cns[0:1, 1:2]))[-1],
                        waits=[("sDVE", fz)], inc="sACT", tag="ep rstd")
                    mrd = sch.emit("dve", lambda v: v.tensor_tensor(
                        R_mr, R_mu, R_rstd, ALU.mult),
                        waits=[("sACT", rs)], inc="sDVE", tag="ep mr")
                    bc = sch.emit("pe", lambda te: (
                        te.matmul(pP[:, :], self.ones_r[0:1, :], R_rstd,
                                  start=True, stop=True),
                        te.matmul(pQ[:, :], self.ones_r[0:1, :], R_mr,
                                  start=True, stop=True))[-1],
                        waits=[("sDVE", mrd)], inc="sPE", tag="ep bc")
                    y1 = sch.emit("dve", lambda v, t=t: (
                        v.scalar_tensor_tensor(x0t(), stt(t), 1.0,
                                               rep_c(pP[:, :], CH),
                                               ALU.mult, ALU.mult),
                        v.tensor_tensor(t3(t1b[:, 0:], TA, CH, TA), x0t(),
                                        rep_c(pQ[:, :], CH), ALU.subtract))[-1],
                        waits=[("sPE", bc), ("sSCR", prev_y1out)],
                        inc="sDVE", tag="y1")
                    prev_y1 = y1
                    prev_y1out = sch.emit("sp", lambda sy, t=t: sy.dma_start(
                        self.h3(self.y1d)[:, :, t * TA:(t + 1) * TA],
                        t3(t1b[:, 0:], TA, CH, TA)),
                        waits=[("sDVE", y1)], inc="sSCR", by=16, tag="y1out")
                self.a_scr_done = sch.c("sSCR")
            else:
                prev_store = 0
                for t in range(NT):
                    ld = sch.emit("sp", lambda sy, t=t: sy.dma_start(
                        x0t(), xs3[:, :, t * TA:(t + 1) * TA]),
                        waits=[("sDVE", twist_done[NT - 1]), ("sOUT", prev_store)],
                        inc="sIN", by=16, tag=f"out x2 t{t}")
                    x3 = sch.emit("dve", lambda v, t=t: v.scalar_tensor_tensor(
                        x0t(), stt(t), 0.5, x0t(), ALU.mult, ALU.add),
                        waits=[("sIN", ld)], inc="sDVE", tag=f"x3 t{t}")
                    prev_store = sch.emit("sp", lambda sy, t=t: sy.dma_start(
                        self.h3(self.yT)[:, :, t * TA:(t + 1) * TA], x0t()),
                        waits=[("sDVE", x3)], inc="sOUT", by=16, tag="yout")
            sch.flush()

    def phase_b(self):
        nc, sch = self.nc, self.sch
        st = contextlib.ExitStack()
        with st:
            En = st.enter_context
            wqb = En(nc.sbuf_tensor("wqb", [P, 8 * 1024], BF16))
            wkb = En(nc.sbuf_tensor("wkb", [P, 8 * 1024], BF16))
            wvb = En(nc.sbuf_tensor("wvb", [P, 8 * 1024], BF16))
            y1t = [En(nc.sbuf_tensor(f"y1t{i}", [P, CH * TA], BF16)) for i in (0, 1)]
            qst = [En(nc.sbuf_tensor(f"qst{i}", [P, CH * TA], BF16)) for i in (0, 1)]
            pK = En(nc.psum_tensor("pK", [P, 512], F32))
            pK2 = En(nc.psum_tensor("pK2", [P, 512], F32))
            pV = En(nc.psum_tensor("pV", [P, 512], F32))
            pV2 = En(nc.psum_tensor("pV2", [P, 512], F32))

            bstage = qst[0].ap().bitcast(F32)
            self.stage_free = [0, 0]
            wq_d = self.load_w(wqb, self.wqT, 1024, bstage)
            wk_d = self.load_w(wkb, self.wkT, 1024, bstage)
            wv_d = self.load_w(wvb, self.wvT, 1024, bstage)

            pstep0 = self.V_sb.ap().ap[0][0]
            ones_set = sch.emit("dve", lambda v: v.memset(
                AP(self.V_sb, 64, [[pstep0, P], [1040, 32], [65, 16], [1, 1]]),
                1.0), inc="sDVE", tag="V ones")

            y1h = self.h3(self.y1d)
            qh = self.h3(self.qd)
            last_pe_on_buf = {}
            qout = {}
            lastKQev = 0
            lastVev = 0
            for t in range(NT):
                tb = t % 2
                w = [("sSCR", self.a_scr_done), ("sPE", last_pe_on_buf.get(t - 2, 0))]
                ld = sch.emit("sp", lambda sy, t=t, tb=tb: sy.dma_start(
                    t3(y1t[tb][:, 0:], TA, CH, TA), y1h[:, :, t * TA:(t + 1) * TA]),
                    waits=w, inc="sIN", by=16, tag=f"B y1 t{t}")
                evQlast = 0
                for (wname, wb, wd, dst_k) in (("k", wkb, wk_d, True),
                                               ("q", wqb, wq_d, False)):
                    for o in range(CH):
                        bank = (pK, pK2)[o % 2]

                        def proj(te, wb=wb, o=o, bank=bank, tb=tb):
                            for k in range(CH):
                                i = te.matmul(
                                    bank[:, :],
                                    wb[:, k * 1024 + o * P: k * 1024 + o * P + P],
                                    y1t[tb][:, k * TA:(k + 1) * TA],
                                    start=(k == 0), stop=(k == CH - 1))
                            return i
                        w = [("sIN", ld), ("sDVE", wd), ("sACT", lastKQev)]
                        mm = sch.emit("pe", proj, waits=w, inc="sPE",
                                      tag=f"B {wname} o{o}")
                        if dst_k:
                            lastKQev = sch.emit(
                                "act", lambda s, o=o, bank=bank, t=t:
                                s.activation(
                                    self.K_sb[:, o * S + t * TA:
                                              o * S + t * TA + TA],
                                    bank[:, :], AF.Copy),
                                waits=[("sPE", mm)], inc="sACT", tag="B kev")
                        else:
                            lastKQev = sch.emit(
                                "act", lambda s, o=o, bank=bank, tb=tb:
                                s.activation(
                                    qst[tb][:, o * TA:(o + 1) * TA],
                                    bank[:, :], AF.Copy),
                                waits=[("sPE", mm), ("sDVE", wv_d),
                                       ("sSCR", qout.get(t - 2, 0))],
                                inc="sACT", tag="B qev")
                            evQlast = lastKQev
                qout[t] = sch.emit("sp", lambda sy, t=t, tb=tb: sy.dma_start(
                    qh[:, :, t * TA:(t + 1) * TA], t3(qst[tb][:, 0:], TA, CH, TA)),
                    waits=[("sACT", evQlast)], inc="sSCR", by=16, tag="B qout")
                for sub in range(4):
                    for half in range(2):
                        bank = (pV, pV2)[half]

                        def vproj(te, sub=sub, half=half, bank=bank, tb=tb):
                            for k in range(CH):
                                i = te.matmul(
                                    bank[:, :],
                                    y1t[tb][:, k * TA + sub * P: k * TA + sub * P + P],
                                    wvb[:, k * 1024 + half * 512: k * 1024 + half * 512 + 512],
                                    start=(k == 0), stop=(k == CH - 1))
                            return i
                        w = [("sIN", ld), ("sDVE", wv_d), ("sACT", lastVev)]
                        mm = sch.emit("pe", vproj, waits=w, inc="sPE",
                                      tag=f"B v s{sub}h{half}")
                        tr = t * 4 + sub
                        pstep = self.V_sb.ap().ap[0][0]
                        dstap = AP(self.V_sb, tr * 1040 + half * 8 * 65,
                                   [[pstep, P], [65, 8], [1, 64]])
                        lastVev = sch.emit(
                            "act", lambda s, dstap=dstap, bank=bank:
                            s.activation(
                                dstap,
                                bank.ap().rearrange("p (h d) -> p h d", d=64),
                                AF.Copy),
                            waits=[("sPE", mm)], inc="sACT", tag="B vev")
                last_pe_on_buf[t] = sch.c("sPE")
            self.b_done_act = sch.c("sACT")
            self.b_done_scr = sch.c("sSCR")
            self.b_vones = ones_set
            sch.flush()

    def phase_c(self):
        nc, sch = self.nc, self.sch
        st = contextlib.ExitStack()
        with st:
            En = st.enter_context
            wob = En(nc.sbuf_tensor("wob", [P, 8 * 1024], BF16))
            qbuf = [En(nc.sbuf_tensor(f"qb{i}", [P, QT], BF16)) for i in (0, 1)]
            wbuf = [En(nc.sbuf_tensor(f"wbf{i}", [P, QT], BF16)) for i in range(4)]
            o2 = En(nc.sbuf_tensor("o2", [P, QT], F32))
            rrow = En(nc.sbuf_tensor("rrow", [1, QT], F32))
            attn = En(nc.sbuf_tensor("attn", [P, CH * QT], BF16))
            x1p = [En(nc.sbuf_tensor(f"x1p{i}", [P, 512], F32)) for i in range(2)]
            x2p = [En(nc.sbuf_tensor(f"x2p{i}", [P, 512], F32)) for i in range(2)]
            x2bf = [En(nc.sbuf_tensor(f"x2bf{i}", [P, 512], BF16)) for i in (0, 1)]
            x2sq = [En(nc.sbuf_tensor(f"x2sq{i}", [P, 512], BF16)) for i in (0, 1)]
            rowC = En(nc.sbuf_tensor("rowC", [1, 2 * TA], F32))
            wstC = En(nc.sbuf_tensor("wstC", [P, 2048], F32))
            psH = En(nc.psum_tensor("psH", [P, 1024], F32))
            psG = En(nc.psum_tensor("psG", [P, 1024], F32))
            pw = [En(nc.psum_tensor(f"pw{i}", [P, 512], F32)) for i in range(4)]

            self.stage_free = [0, 0]
            wo_d = self.load_w(wob, self.woT, 1024, wstC.ap())
            qh = self.h3(self.qd)
            x1h = self.h3(self.x1d)
            x2h = self.h3(self.x2d)

            qld = {}
            rowc_out = 0
            slot_free = {}
            sig_cnt = {}
            wv_done = {}
            norm_done = {}
            wo_last_mm = 0
            x1_ld = {}
            x2_st = {}
            stat_fin = {}

            def karea(cp, sk, hi):
                off = cp * S + sk * P
                return self.K_sb[64 * hi:64 * hi + 64, off:off + P]

            def emit_qld(gidx):
                gq, gcp = divmod(gidx, NPAIR)
                if gq >= NQT or (gq, gcp) in qld:
                    return
                qb = qbuf[gidx % 2]
                w = [("sSCR", self.b_done_scr)]
                if gidx - 2 >= 0:
                    pq, pcp = divmod(gidx - 2, NPAIR)
                    assert (pq, pcp) in wv_done, "qld prefetch ordering"
                    w.append(("sPE", wv_done[(pq, pcp)]))
                qld[(gq, gcp)] = sch.emit(
                    "sp", lambda sy, gq=gq, gcp=gcp, qb=qb: sy.dma_start(
                        qb[:, :], qh[:, gcp, gq * QT:(gq + 1) * QT]),
                    waits=w, inc="sIN", by=16, tag=f"C q{gq} p{gcp}")

            emit_qld(0)
            emit_qld(1)
            for q in range(NQT):
                for cp in range(NPAIR):
                    idx = q * NPAIR + cp
                    qb = qbuf[idx % 2]
                    for sk in range(SKN):
                        if self.paranoid:
                            par = [("sPE", sch.c("sPE")),
                                   ("sACT", sch.c("sACT")),
                                   ("sDVE", sch.c("sDVE"))]
                        else:
                            par = []
                        wset = (sk % 2) * 2
                        w = [("sIN", qld[(q, cp)]), ("sACT", self.b_done_act),
                             ("sDVE", self.b_vones)]
                        if sk > 0:
                            w.append(("sACT", sig_cnt[(q, cp, sk - 1, 1)]))
                        elif idx > 0:
                            pq, pcp = divmod(idx - 1, NPAIR)
                            w.append(("sACT", sig_cnt[(pq, pcp, SKN - 1, 1)]))
                        if sk == 0 and cp == 0 and q > 0:
                            w.append(("sDVE", prev_q_dve))

                        def scores(te, cp=cp, sk=sk, qb=qb):
                            te.matmul(psH[:, 0:512], karea(cp, sk, 0),
                                      qb[0:64, 0:512], start=True, stop=True,
                                      tile_position=(0, 0))
                            te.matmul(psG[:, 0:512], karea(cp, sk, 1),
                                      qb[64:128, 0:512], start=True, stop=True,
                                      tile_position=(64, 0))
                            ih = te.matmul(psH[:, 512:1024], karea(cp, sk, 0),
                                           qb[0:64, 512:1024], start=True, stop=True,
                                           tile_position=(0, 0))
                            ig = te.matmul(psG[:, 512:1024], karea(cp, sk, 1),
                                           qb[64:128, 512:1024], start=True, stop=True,
                                           tile_position=(64, 0))
                            return (ih, ig)

                        def sc_emit(te, f=scores):
                            ih, ig = f(te)
                            return ig
                        mmsc = sch.emit("pe", sc_emit, waits=w + par,
                                        inc="sPE", tag=f"C sc q{q}p{cp}s{sk}")
                        if sk == 0:
                            emit_qld(idx + 2 - 1) if False else None
                        for hi, (ps, wb_i) in enumerate(((psH, wset), (psG, wset + 1))):
                            w = [("sPE", mmsc)]
                            if sk >= 2:
                                w.append(("sPE", wv_done.get((q, cp, sk - 2, hi), 0)))
                            elif idx > 0:
                                pq, pcp = divmod(idx - 1, NPAIR)
                                w.append(("sPE",
                                          wv_done.get((pq, pcp, SKN - 2 + sk, hi), 0)))

                            def sig(s, ps=ps, wb_i=wb_i):
                                return s.activation(wbuf[wb_i][:, :], ps[:, :],
                                                    AF.Sigmoid, scale=self.cns[:, 0:1])
                            sig_cnt[(q, cp, sk, hi)] = sch.emit(
                                "act", sig, waits=w, inc="sACT",
                                tag=f"C sig{hi}")
                        for hi in range(2):
                            h = 2 * cp + hi
                            wb_i = wset + hi
                            w = [("sACT", sig_cnt[(q, cp, sk, hi)])]
                            if sk == 0 and idx > 0:
                                pq, pcp = divmod(idx - 1, NPAIR)
                                w.append(("sDVE", norm_done.get((pq, pcp, hi), 0)))

                            def wv(te, h=h, sk=sk, wb_i=wb_i, hi=hi):
                                voff = sk * 1040 + h * 65
                                te.matmul(pw[2 * hi][0:65, :],
                                          self.V_sb[:, voff:voff + 65],
                                          wbuf[wb_i][:, 0:512],
                                          start=(sk == 0), stop=(sk == SKN - 1),
                                          skip_group_check=True)
                                return te.matmul(pw[2 * hi + 1][0:65, :],
                                                 self.V_sb[:, voff:voff + 65],
                                                 wbuf[wb_i][:, 512:1024],
                                                 start=(sk == 0), stop=(sk == SKN - 1),
                                                 skip_group_check=True)
                            wv_done[(q, cp, sk, hi)] = sch.emit(
                                "pe", wv, waits=w, inc="sPE", tag=f"C wv{hi}")
                    wv_done[(q, cp)] = sch.c("sPE")
                    emit_qld(idx + 2)
                    rbc_prev = 0
                    for hi in range(2):
                        pa, pb = pw[2 * hi], pw[2 * hi + 1]

                        def evhead(v, pa=pa, pb=pb):
                            v.tensor_scalar(rrow[0:1, 0:512], pa[64:65, :],
                                            EPS_AT, None, ALU.add)
                            v.tensor_scalar(rrow[0:1, 512:1024], pb[64:65, :],
                                            EPS_AT, None, ALU.add)
                            v.reciprocal(rrow[0:1, :], rrow[0:1, :])
                            v.tensor_copy(o2[0:64, 0:512], pa[0:64, :])
                            return v.tensor_copy(o2[0:64, 512:1024], pb[0:64, :])
                        evh = sch.emit("dve", evhead,
                                       waits=[("sPE", wv_done[(q, cp, SKN - 1, hi)]),
                                              ("sPE", rbc_prev)],
                                       inc="sDVE", tag=f"C evh{hi}")
                        rbc = sch.emit("pe", lambda te, pa=pa, pb=pb: (
                            te.matmul(pa[0:64, :], self.ones_r[0:1, 0:64],
                                      rrow[0:1, 0:512], start=True, stop=True),
                            te.matmul(pb[0:64, :], self.ones_r[0:1, 0:64],
                                      rrow[0:1, 512:1024], start=True, stop=True))[-1],
                            waits=[("sDVE", evh)], inc="sPE", tag=f"C rbc{hi}")
                        rbc_prev = rbc
                        rsl = slice(64 * hi, 64 * hi + 64)

                        def norm(v, pa=pa, pb=pb, rsl=rsl, cp=cp):
                            v.scalar_tensor_tensor(
                                attn[rsl, cp * QT: cp * QT + 512],
                                o2[0:64, 0:512], 1.0, pa[0:64, :],
                                ALU.mult, ALU.mult)
                            return v.scalar_tensor_tensor(
                                attn[rsl, cp * QT + 512: cp * QT + 1024],
                                o2[0:64, 512:1024], 1.0, pb[0:64, :],
                                ALU.mult, ALU.mult)
                        w = [("sPE", rbc)]
                        if q > 0:
                            w.append(("sPE", wo_last_mm))
                        norm_done[(q, cp, hi)] = sch.emit(
                            "dve", norm, waits=w, inc="sDVE", tag=f"C norm{hi}")
                if self.stop and self.stop.startswith("att") and q == int(self.stop[3:]):
                    wdump = [("sDVE", norm_done[(q, cp2, hi)])
                             for cp2 in range(NPAIR) for hi in range(2)]
                    sch.emit("sp", lambda sy: sy.dma_start(
                        self.attnd.ap().rearrange("(c p) s -> p c s", p=P),
                        t3(attn[:, 0:], QT, CH, QT)),
                        waits=wdump, inc="sSCR", by=16, tag="attdump")
                    cpd = sch.emit("dve", lambda v: (
                        v.tensor_copy(o2[0:65, 0:512], pw[2][0:65, :]),
                        v.tensor_copy(o2[0:65, 512:1024], pw[3][0:65, :]))[-1],
                        waits=wdump, inc="sDVE", tag="wvdump-cp")
                    sch.emit("sp", lambda sy: sy.dma_start(
                        self.wvd.ap()[:, :], o2[:, :]),
                        waits=[("sDVE", cpd)], inc="sSCR", by=16, tag="wvdump")
                    def delay(v):
                        for _ in range(20):
                            i = v.tensor_copy(o2[:, :], o2[:, :])
                        return i
                    dl = sch.emit("dve", delay, waits=[("sDVE", cpd)],
                                  inc="sDVE", tag="delay")
                    sch.emit("sp", lambda sy: sy.dma_start(
                        self.attnd2.ap().rearrange("(c p) s -> p c s", p=P),
                        t3(attn[:, 0:], QT, CH, QT)),
                        waits=[("sDVE", dl)], inc="sSCR", by=16, tag="attdump2")
                    self.c_scr_done = sch.c("sSCR")
                    sch.flush()
                    return
                attn_ready = [("sDVE", norm_done[(q, cp2, hi)])
                              for cp2 in range(NPAIR) for hi in range(2)]
                for half in range(2):
                    for o in range(CH):
                        f = q * 16 + half * CH + o
                        bank_sl = (slice(0, 512), slice(512, 1024))[o % 2]

                        def womm(te, o=o, half=half, bank_sl=bank_sl):
                            for k in range(CH):
                                i = te.matmul(
                                    psH[:, bank_sl],
                                    wob[:, k * 1024 + o * P: k * 1024 + o * P + P],
                                    attn[:, k * QT + half * 512: k * QT + half * 512 + 512],
                                    start=(k == 0), stop=(k == CH - 1))
                            return i
                        w = list(attn_ready) + [
                            ("sDVE", wo_d),
                            ("sACT", sig_cnt[(q, NPAIR - 1, SKN - 1, 1)])]
                        if f - 2 in slot_free:
                            w.append(("sDVE", slot_free[f - 2][1]))
                        mm = sch.emit("pe", womm, waits=w, inc="sPE",
                                      tag=f"C wo q{q}h{half}o{o}")
                        wo_last_mm = mm
                        j = f % 2
                        jb = f % 2
                        wx = [("sSCR", self.a_scr_done)]
                        if f - 2 in slot_free:
                            wx.append(("sDVE", slot_free[f - 2][1]))
                        ldx = sch.emit("sp", lambda sy, o=o, half=half, q=q, j=j:
                                       sy.dma_start(x1p[j][:, :],
                                                    x1h[:, o, q * QT + half * 512:
                                                        q * QT + half * 512 + 512]),
                                       waits=wx, inc="sIN", by=16, tag="C x1")
                        w2 = [("sPE", mm), ("sIN", ldx)]
                        if f - 2 in slot_free:
                            w2 += [("sSCR", slot_free[f - 2][0]),
                                   ("sACT", slot_free[f - 2][2]),
                                   ("sPE", slot_free[f - 2][3])]
                        x2c = sch.emit("dve", lambda v, j=j, bank_sl=bank_sl:
                                       v.tensor_tensor(x2p[j][:, :], psH[:, bank_sl],
                                                       x1p[j][:, :], ALU.add),
                                       waits=w2, inc="sDVE", tag="C x2")
                        x2o = sch.emit("sp", lambda sy, o=o, half=half, q=q, j=j:
                                       sy.dma_start(x2h[:, o, q * QT + half * 512:
                                                        q * QT + half * 512 + 512],
                                                    x2p[j][:, :]),
                                       waits=[("sDVE", x2c)], inc="sSCR", by=16,
                                       tag="C x2out")
                        wcb = [("sDVE", x2c)]
                        if f - 2 in slot_free:
                            wcb.append(("sPE", slot_free[f - 2][3]))
                        cst = sch.emit("dve", lambda v, j=j, jb=jb: v.tensor_copy(
                            x2bf[jb][:, :], x2p[j][:, :]),
                            waits=wcb, inc="sDVE", tag="C x2bf")
                        sqq = sch.emit("act", lambda s, j=j, jb=jb: s.activation(
                            x2sq[jb][:, :], x2p[j][:, :], AF.Square),
                            waits=wcb, inc="sACT", tag="C x2sq")

                        def statmm(te, o=o, jb=jb):
                            te.matmul(psG[0:1, 0:512], self.ones_k[:, :],
                                      x2bf[jb][:, :], start=(o == 0),
                                      stop=(o == CH - 1), skip_group_check=True)
                            return te.matmul(psG[0:1, 512:1024], self.ones_k[:, :],
                                             x2sq[jb][:, :], start=(o == 0),
                                             stop=(o == CH - 1),
                                             skip_group_check=True)
                        ws = [("sDVE", cst), ("sACT", sqq)]
                        if o == 0 and half == 1:
                            ws.append(("sDVE", stat_fin[(q, 0)]))
                        smm = sch.emit("pe", statmm, waits=ws,
                                       inc="sPE", tag="C stat")
                        slot_free[f] = (x2o, x2c, sqq, smm)
                    RC_mu = rowC[0:1, 0:512]
                    RC_var = rowC[0:1, 512:1024]

                    def finh(v):
                        v.tensor_scalar(RC_mu, psG[0:1, 0:512],
                                        1.0 / D, None, ALU.mult)
                        v.tensor_scalar(RC_var, psG[0:1, 512:1024],
                                        1.0 / D, EPS_LN, ALU.mult, ALU.add)
                        v.tensor_tensor(rrow[0:1, 0:512], RC_mu, RC_mu, ALU.mult)
                        return v.tensor_tensor(RC_var, RC_var, rrow[0:1, 0:512],
                                               ALU.subtract)
                    wfin = [("sPE", sch.c("sPE")), ("sSCR", rowc_out)]
                    stat_fin[(q, half)] = sch.emit(
                        "dve", finh, waits=wfin,
                        inc="sDVE", tag=f"C fin q{q}h{half}")
                    sch.emit("sp", lambda sy, q=q, half=half: sy.dma_start(
                        self.mu3d[0:1, q * QT + half * 512: q * QT + half * 512 + 512],
                        RC_mu), waits=[("sDVE", stat_fin[(q, half)])],
                        inc="sSCR", by=16, tag="C mu3out")
                    rowc_out = sch.emit("sp", lambda sy, q=q, half=half: sy.dma_start(
                        self.var3d[0:1, q * QT + half * 512: q * QT + half * 512 + 512],
                        RC_var), waits=[("sDVE", stat_fin[(q, half)])],
                        inc="sSCR", by=16, tag="C var3out")
                prev_q_dve = sch.c("sDVE")
            self.c_scr_done = sch.c("sSCR")
            sch.flush()

    def build(self):
        nc = self.nc
        self.declare_hbm()
        ctx = contextlib.ExitStack()
        with ctx:
            E_ = ctx.enter_context
            self.alloc_common(E_)
            self.p0()
            if "A" in self.phases:
                with nc.named_scope("phA"):
                    self.ffn("A", self.xT)
            if "B" in self.phases or "C" in self.phases:
                bc_scope = contextlib.ExitStack()
                with bc_scope:
                    self.K_sb = bc_scope.enter_context(
                        nc.sbuf_tensor("K_sb", [P, CH * S], BF16))
                    self.V_sb = bc_scope.enter_context(
                        nc.sbuf_tensor("V_sb", [P, 32 * 1040], BF16))
                    if "B" in self.phases:
                        with nc.named_scope("phB"):
                            self.phase_b()
                    if "C" in self.phases:
                        with nc.named_scope("phC"):
                            self.phase_c()
            if "E" in self.phases:
                with nc.named_scope("phE"):
                    self.ffn("E", self.x2d)
            sch = self.sch
            sch.emit("sp", lambda sy: sy.nop(),
                     waits=[("sSCR", sch.c("sSCR")), ("sOUT", sch.c("sOUT")),
                            ("sIN", sch.c("sIN")), ("sGP", sch.c("sGP"))],
                     tag="fence")
            sch.flush()
        return nc


def build_program(phases="ABCE", dbg=()):
    mk = MK(phases, dbg)
    nc = mk.build()
    return nc, mk.sch.ninst


def _prep_core_inputs(inp, b):
    x = np.asarray(inp["L0_attn_x"])

    def f32c(a):
        return np.ascontiguousarray(np.asarray(a, dtype=np.float32))
    m = {
        "xT": f32c(x[b].T),
        "wqT": f32c(np.asarray(inp["L0_attn_Wq"]).T),
        "wkT": f32c(np.asarray(inp["L0_attn_Wk"]).T),
        "wvT": f32c(np.asarray(inp["L0_attn_Wv"]).T),
        "woT": f32c(np.asarray(inp["L0_attn_Wo"]).T),
        "cw1": f32c(inp["L0_ffn_cn_w1"]),
        "cw2": f32c(inp["L0_ffn_cn_w2"]),
        "cw3": f32c(inp["L0_ffn_cn_w3"]),
        "cw4": f32c(np.asarray(inp["L0_ffn_cn_w4"]).reshape(2, 128).T),
        "gw1": f32c(inp["L0_ffn_gc_w1"]),
        "gw2": f32c(inp["L0_ffn_gc_w2"]),
        "gw3": f32c(np.asarray(inp["L0_ffn_gc_w3"]).reshape(2, 128).T),
        "cb1": f32c(np.asarray(inp["L0_ffn_cn_b1"]).reshape(8, 128).T),
        "cb2": f32c(np.asarray(inp["L0_ffn_cn_b2"]).reshape(4, 128).T),
        "cb3": f32c(np.asarray(inp["L0_ffn_cn_b3"]).reshape(2, 128).T),
        "cb4": f32c(np.asarray(inp["L0_ffn_cn_b4"]).reshape(1, 1)),
        "gb1": f32c(np.asarray(inp["L0_ffn_gc_b1"]).reshape(4, 128).T),
        "gb2": f32c(np.asarray(inp["L0_ffn_gc_b2"]).reshape(2, 128).T),
        "gb3": f32c(np.asarray(inp["L0_ffn_gc_b3"]).reshape(1, 1)),
        "ar": f32c(np.asarray(inp["L0_ffn_adaptive_range"]).reshape(1, 1)),
        "cns": np.tile(np.array([[0.125, -0.5, 0.5, 0.0]], dtype=np.float32),
                       (128, 1)),
    }
    return m


def kernel(**inputs):
    nc, _ = build_program()
    in_maps = [_prep_core_inputs(inputs, b) for b in range(8)]
    res = run_bass_kernel_spmd(nc, in_maps, core_ids=list(range(8)))
    out = np.stack([np.asarray(res.results[b]["yT"]).T for b in range(8)], axis=0)
    return np.ascontiguousarray(out.astype(np.float32, copy=False))
